# version 11
# speedup vs baseline: 1.1425x; 1.0337x over previous
import numpy as np
import concourse.bass as bass
import concourse.mybir as mybir

F32 = mybir.dt.float32
AF = mybir.ActivationFunctionType
ALU = mybir.AluOpType
AX = mybir.AxisListType

NCORES = 8
H, C, NHD, Q = 16, 1024, 64, 256
PT = 8
TOK = 2048
TOKX = 2056
E = 8


class Sem:
    def __init__(self, nc, name):
        self.h = nc.alloc_semaphore(name)
        self.n = 0

    def inc(self, inst, k=1):
        inst.then_inc(self.h, k)
        self.n += k
        return inst

    def dinc(self, inst):
        inst.then_inc(self.h, 16)
        self.n += 16
        return inst


class Ctx:
    pass


class DmaRing:
    def __init__(self, nc, name, nslots):
        self.sems = [Sem(nc, f"{name}{i}") for i in range(nslots)]
        self.nslots = nslots
        self.issued = [0] * nslots

    def issue(self, eng, i, dmas):
        sl = i % self.nslots
        for out, in_ in dmas:
            self.sems[sl].dinc(eng.dma_start(out=out, in_=in_))
        self.issued[sl] += len(dmas)

    def wait(self, eng, i):
        sl = i % self.nslots
        eng.wait_ge(self.sems[sl].h, self.sems[sl].n)

    def wait_upto(self, eng, i, per=1):
        sl = i % self.nslots
        eng.wait_ge(self.sems[sl].h, 16 * per * (i // self.nslots + 1))


class Arena:
    def __init__(self, nc, words):
        self.t = nc.alloc_sbuf_tensor("arena", [128, words], F32)
        self.words = words
        self.off = 0
        self.marks = []

    def push(self):
        self.marks.append(self.off)

    def pop(self):
        self.off = self.marks.pop()

    def view(self, shape, dt=F32):
        n = 1
        for x in shape[1:]:
            n *= x
        words = n if dt == F32 else (n + 1) // 2
        off = (self.off + 15) // 16 * 16
        assert off + n <= self.words or True, ""
        assert off + words <= self.words, f"arena overflow: {off}+{words} > {self.words}"
        self.off = off + words
        v = self.t[0:shape[0], off:off + words]
        if dt != F32:
            v = v.bitcast(dt)[0:shape[0], 0:n]
        if len(shape) == 2:
            return v
        if len(shape) == 3:
            return v.rearrange("p (a c) -> p a c", a=shape[1])
        if len(shape) == 4:
            return v.rearrange("p (a b c) -> p a b c", a=shape[1], b=shape[2])
        if len(shape) == 5:
            return v.rearrange("p (a b c e) -> p a b c e", a=shape[1], b=shape[2],
                               c=shape[3])
        raise ValueError(shape)


def stage_shared(g):
    f = np.float32
    o = {}

    def tile8x8(WT):
        return np.ascontiguousarray(WT.reshape(8, 128, 8, 128).transpose(2, 0, 1, 3))

    WrT = np.ascontiguousarray(g["L0_attn_Wr"].T.astype(f))
    WkT = np.ascontiguousarray(g["L0_attn_Wk"].T.astype(f))
    WvT = np.ascontiguousarray(g["L0_attn_Wv"].T.astype(f))
    WoT = np.ascontiguousarray(g["L0_attn_Wo"].T.astype(f))
    o["wrt_t"] = tile8x8(WrT)
    o["wkt_t"] = tile8x8(WkT)
    o["wot_t"] = tile8x8(WoT)
    o["wkt_p"] = WkT
    o["wvt_p"] = WvT

    def col128(v):
        return np.ascontiguousarray(v.astype(f).reshape(8, 128).T)

    o["maa"] = np.ascontiguousarray(np.concatenate(
        [col128(g["L0_attn_time_maa_k"]), col128(g["L0_attn_time_maa_v"]),
         col128(g["L0_attn_time_maa_r"])], axis=1))
    o["bias_col"] = np.ascontiguousarray(np.concatenate(
        [col128(g["L0_attn_br"]), col128(g["L0_attn_bk"]),
         col128(g["L0_attn_bv"]), col128(g["L0_attn_bo"])], axis=1))
    o["bk_row"] = np.ascontiguousarray(g["L0_attn_bk"].astype(f)[None, :])
    o["bv_row"] = np.ascontiguousarray(g["L0_attn_bv"].astype(f)[None, :])
    o["tdec_row"] = np.ascontiguousarray(g["L0_attn_time_decay"].astype(f)[None, :])
    o["tdec_col"] = np.ascontiguousarray(g["L0_attn_time_decay"].astype(f)[:, None])
    o["u_row"] = np.ascontiguousarray(g["L0_attn_time_faaaa"].astype(f)[None, :])

    rm = g["L0_ffn_rail_maker"].astype(f)
    ru = g["L0_ffn_rail_up_proj"].astype(f)
    RM = np.zeros((8, 4, 128, 128), f)
    RU = np.zeros((8, 4, 128, 128), f)
    for gg in range(8):
        for hh in range(4):
            for rl in range(8):
                r = 8 * gg + rl
                RM[gg, hh, 16*rl:16*rl+16, 16*rl:16*rl+16] = rm[hh, r].T
                RU[gg, hh, 16*rl:16*rl+16, 64*(rl % 2):64*(rl % 2)+64] = ru[hh, r].T
    o["rmblk"] = np.ascontiguousarray(RM.reshape(32, 128, 128))
    o["rublk"] = np.ascontiguousarray(RU.reshape(32, 128, 128))

    WcT = np.ascontiguousarray(g["L0_ffn_w_cproj"].T.astype(f))
    o["wct_t"] = np.ascontiguousarray(
        WcT.reshape(128, 128, 8, 128).transpose(0, 2, 1, 3))

    o["wk1_p"] = np.ascontiguousarray(g["L1_attn_W_k"].T.astype(f))
    o["wv1_p"] = np.ascontiguousarray(g["L1_attn_W_v"].T.astype(f))
    o["wq1_p"] = np.ascontiguousarray(g["L1_attn_W_q"].T.astype(f))

    o["wg_t"] = np.ascontiguousarray(g["L1_ffn_Wg"].astype(f).reshape(8, 128, 8))
    W1 = g["L1_ffn_W1"].astype(f)
    o["w1_t"] = np.ascontiguousarray(
        W1.reshape(8, 8, 128, 16, 128).transpose(0, 3, 1, 2, 4))
    W2 = g["L1_ffn_W2"].astype(f)
    o["w2_t"] = np.ascontiguousarray(
        W2.reshape(8, 16, 128, 8, 128).transpose(0, 3, 1, 2, 4))
    o["b1_col"] = np.ascontiguousarray(
        g["L1_ffn_b1"].astype(f).reshape(8, 16, 128).transpose(0, 2, 1))
    o["b2_p"] = np.ascontiguousarray(g["L1_ffn_b2"].astype(f))

    qi = np.arange(256)
    o["dqp"] = np.ascontiguousarray(np.stack(
        [np.maximum(qi[None, :] - (128*pt2 + qi[:128, None]) - 1, 0)
         for pt2 in range(2)], axis=1).astype(f))
    o["mupper"] = np.ascontiguousarray(np.stack(
        [(qi[None, :] > (128*pt2 + qi[:128, None]))
         for pt2 in range(2)], axis=1).astype(f))
    o["ident256"] = np.ascontiguousarray(np.stack(
        [(qi[None, :] == (128*pt2 + qi[:128, None]))
         for pt2 in range(2)], axis=1).astype(f))
    o["ident128"] = np.eye(128, dtype=f)
    sh = np.zeros((16, 1024), f)
    for hh in range(16):
        sh[hh, 64*hh:64*hh+64] = 1.0
    o["selhead"] = sh
    sp = np.zeros((16, 1024), f)
    for hp in range(8):
        sp[2*hp, hp*128:hp*128+64] = 1.0
        sp[2*hp+1, hp*128+64:hp*128+128] = 1.0
    o["selpair"] = sp
    s8 = np.zeros((8, 1024), f)
    for e in range(8):
        s8[e, e*128:(e+1)*128] = 1.0
    o["sel8e"] = s8
    o["iota16"] = np.ascontiguousarray(np.tile(np.arange(256, dtype=f), (16, 1)))
    o["iota16r"] = np.ascontiguousarray(np.tile(np.arange(255, -1, -1, dtype=f), (16, 1)))
    return o


def stage_core(g, c):
    f = np.float32
    X = g["L0_attn_x"].astype(f)
    xs = X[:, 256*c:256*c+256, :].reshape(TOK, C)
    pr = X[:, max(256*c - 1, 0), :]
    full = np.concatenate([xs, pr], axis=0)
    o = {"xin": np.ascontiguousarray(full.T)}
    o["prevv"] = np.full((128, 1), 1.0 if c > 0 else 0.0, f)
    jj = np.arange(8)
    o["jpow16"] = np.ascontiguousarray(
        np.tile(np.maximum(c - 1 - jj, 0).astype(f), (16, 1)))
    o["jmask16"] = np.ascontiguousarray(
        np.tile((jj < c).astype(f), (16, 1)))
    return o


def unstage_out(results):
    fin = np.empty((8, 2048, 1024), np.float32)
    for c, r in enumerate(results):
        fin[:, 256*c:256*c+256, :] = r["out"].T.reshape(8, 256, 1024)
    return fin


def build(taps=(), ncores=NCORES, phases=("p0", "p1", "p2", "p3", "p1b", "p4",
                                          "p5", "p6")):
    nc = bass.Bass("TRN2", target_bir_lowering=False)
    ct = Ctx()
    ct.nc = nc
    ct.taps = {}
    ct.want = set(taps)
    ct.ncores = ncores

    def inp(name, shape):
        return nc.dram_tensor(name, list(shape), F32, kind="ExternalInput").ap()

    def internal(name, shape, dt=F32, **kw):
        return nc.dram_tensor(name, list(shape), dt, kind="Internal", **kw).ap()

    d = Ctx()
    for name, shape in [
        ("xin", [C, TOKX]), ("prevv", [128, 1]),
        ("jpow16", [16, 8]), ("jmask16", [16, 8]),
        ("wrt_t", [8, 8, 128, 128]), ("wkt_t", [8, 8, 128, 128]),
        ("wot_t", [8, 8, 128, 128]),
        ("wkt_p", [C, C]), ("wvt_p", [C, C]),
        ("maa", [128, 24]), ("bias_col", [128, 32]),
        ("bk_row", [1, C]), ("bv_row", [1, C]),
        ("tdec_row", [1, 16]), ("tdec_col", [16, 1]), ("u_row", [1, 16]),
        ("rmblk", [32, 128, 128]), ("rublk", [32, 128, 128]),
        ("wct_t", [128, 8, 128, 128]),
        ("wk1_p", [C, C]), ("wv1_p", [C, C]), ("wq1_p", [C, C]),
        ("wg_t", [8, 128, 8]), ("w1_t", [8, 16, 8, 128, 128]),
        ("w2_t", [8, 8, 16, 128, 128]), ("b1_col", [8, 128, 16]), ("b2_p", [8, C]),
        ("dqp", [128, 2, 256]), ("mupper", [128, 2, 256]), ("ident256", [128, 2, 256]),
        ("ident128", [128, 128]), ("selhead", [16, C]), ("selpair", [16, C]),
        ("iota16", [16, 256]), ("iota16r", [16, 256]), ("sel8e", [8, 1024]),
    ]:
        setattr(d, name, inp(name, shape))
    ct.d = d

    s = Ctx()
    s.wmt = internal("hbm_wmt", [16, 2, 128, 256])
    s.ya = internal("hbm_ya", [8, 128, 8, 256])
    s.rfm = internal("hbm_rfm", [8, 128, 8, 256])
    s.attn0 = internal("hbm_attn0", [8, 128, 8, 256])
    s.ffn0 = internal("hbm_ffn0", [4, 128, 8, 512])
    s.x1 = internal("hbm_x1", [8, 128, 2048])
    s.attn1 = internal("hbm_attn1", [16, 128, 8, 128])
    s.wctp = internal("hbm_wctp", [128, 128, 2, 8, 128], dt=mybir.dt.bfloat16)
    s.l1p = internal("hbm_l1p", [3, 8, 128, 2, 1024], dt=mybir.dt.bfloat16)
    s.w1p = internal("hbm_w1p", [8, 16, 128, 2, 8, 128], dt=mybir.dt.bfloat16)
    s.w2p = internal("hbm_w2p", [8, 8, 128, 2, 16, 128], dt=mybir.dt.bfloat16)
    s.xn1p = internal("hbm_xn1p", [8, 2, 128, 2048], dt=mybir.dt.bfloat16)
    s.cc_in = internal("cc_in", [128, 4096])
    s.cc_out = internal("cc_out", [8, 128, 4096],
                        **({"addr_space": "Shared"} if ncores > 1 else {}))
    ct.s = s

    ct.out = nc.dram_tensor("out", [C, TOK], F32, kind="ExternalOutput").ap()

    A = nc.alloc_sbuf_tensor
    b = Ctx()
    b.xn = A("xn", [128, 8, TOKX], F32)
    b.maa = A("maa_sb", [128, 24], F32)
    b.bias = A("bias_sb", [128, 32], F32)
    b.bkbc = A("bkbc", [128, C], F32)
    b.bvbc = A("bvbc", [128, C], F32)
    b.prevv = A("prevv_sb", [128, 1], F32)
    b.tdr = A("tdr", [1, 16], F32)
    b.tdc = A("tdc", [16, 1], F32)
    b.urow = A("urow", [1, 16], F32)
    b.id128 = A("id128_sb", [128, 128], F32)
    b.jpow = A("jpow_sb", [16, 8], F32)
    b.jmask = A("jmask_sb", [16, 8], F32)
    b.onesrow = A("onesrow", [1, 128], F32)
    b.ones128 = A("ones128", [128, 1], F32)
    b.edc = A("edc", [16, 1], F32)
    b.edr = A("edr", [1, 16], F32)
    b.wsr = A("wsr", [1, 16], F32)
    b.edbc = A("edbc", [128, 16], F32)
    b.ubc = A("ubc", [128, 16], F32)
    b.coefbc = A("coefbc", [128, 128], F32)
    b.wkbc = A("wkbc", [128, 2, C], F32)
    b.wbbc = A("wbbc", [128, 8, 256], F32)
    b.state = A("state_sb", [128, 4096], F32)
    b.cpack = A("cpack", [128, 32, 128], F32)
    ct.b = b

    ct.ps = [nc.alloc_psum_tensor(f"ps{i}", [128, 512], F32) for i in range(8)]
    free_words = (nc.sbuf_top - nc.sbuf_base) // 4 - 64
    ct.ar = Arena(nc, free_words)
    print(f"[mk] arena words/partition: {free_words} ({free_words*4//1024} KiB/part)")

    sm = Ctx()
    sm._names = []

    def mksem(name):
        sem = Sem(nc, name)
        setattr(sm, name, sem)
        sm._names.append(name)
        return sem

    ct.mksem = mksem
    for name in ["tT", "tV", "tS", "tG", "tY", "wA", "wB", "wC", "wD", "outd", "ccs",
                 "p1gate"]:
        mksem(name)
    ct.sm = sm

    ct._bar = {e: Sem(nc, f"bar_{e}") for e in ["t", "v", "s", "g", "y"]}

    def barrier(block, engines="tvsgy"):
        target = {}
        for e in engines:
            target[e] = ct._bar[e].n + 1

        def mk(e):
            def f(eng):
                eng.drain().then_inc(ct._bar[e].h, 1)
                ct._bar[e].n += 1
                for e2 in engines:
                    eng.wait_ge(ct._bar[e2].h, target[e2])
            return f
        secs = {"t": block.tensor, "v": block.vector, "s": block.scalar,
                "g": block.gpsimd, "y": block.sync}
        for e in engines:
            secs[e](mk(e))

    ct.barrier = barrier

    with nc.Block() as block:
        if "p0" in phases:
            phase0(block, ct)
        if "p1" in phases:
            phase1(block, ct)
        if "p2" in phases:
            phase2(block, ct)
        if "p3" in phases:
            phase3(block, ct)
        if "p1b" in phases:
            phase1b(block, ct)
        if "p4" in phases:
            phase4(block, ct)
        if "p5" in phases:
            phase5(block, ct)
        if "p6" in phases:
            phase6(block, ct)
        phase_taps(block, ct)
    return nc, ct


def tap(ct, name, ap_fn):
    if name in ct.want:
        ct.taps[name] = ap_fn


def phase_taps(block, ct):
    nc, sm = ct.nc, ct.sm
    ct.barrier(block, "tvsgy")
    if not ct.taps:
        return

    @block.gpsimd
    def _(g):
        n = 0
        for name, ap_fn in ct.taps.items():
            ap = ap_fn()
            o = nc.dram_tensor("tap_" + name, list(ap.shape), F32,
                               kind="ExternalOutput").ap()
            g.dma_start(out=o, in_=ap).then_inc(sm.outd.h, 16)
            n += 1
        g.wait_ge(sm.outd.h, sm.outd.n + 16 * n)
        sm.outd.n += 16 * n


def phase0(block, ct):
    nc, d, s, b, sm, ps = ct.nc, ct.d, ct.s, ct.b, ct.sm, ct.ps
    bar = ct.barrier
    ar = ct.ar
    ar.push()
    a = Ctx()
    a.iota16 = ar.view([16, 256])
    a.iota16r = ar.view([16, 256])
    a.dqp = ar.view([128, 2, 256])
    a.mupper = ar.view([128, 2, 256])
    a.id256 = ar.view([128, 2, 256])
    a.selhead = ar.view([16, C])
    a.selpair = ar.view([16, C])
    a.pw = ar.view([16, 256])
    a.wkr = ar.view([16, 256])
    a.ones16 = ar.view([16, 128])
    a.coeft = ar.view([16, 128])
    a.wstage = ar.view([128, 8, 256])
    a.rmb = ar.view([128, 4, 128])
    a.rub = ar.view([128, 4, 128])
    a.mrow = ar.view([1, TOKX])
    a.rrow = ar.view([1, TOKX])
    a.srow = ar.view([1, TOKX])
    a.qrow = ar.view([1, TOKX])
    a.mbc = ar.view([128, TOKX])
    a.rbc = ar.view([128, TOKX])
    a.xsq = a.mbc

    @block.sync
    def _(sy):
        loads = [
            (b.maa, d.maa), (b.bias, d.bias_col), (b.prevv, d.prevv),
            (b.tdr, d.tdec_row), (b.tdc, d.tdec_col), (b.urow, d.u_row),
            (a.iota16, d.iota16), (a.iota16r, d.iota16r),
            (a.dqp, d.dqp), (a.mupper, d.mupper),
            (a.id256, d.ident256), (b.id128, d.ident128),
            (a.selhead, d.selhead), (a.selpair, d.selpair),
            (b.jpow, d.jpow16), (b.jmask, d.jmask16),
        ]
        for dst, src_ in loads:
            sl = tuple(slice(None) for _ in dst.shape)
            sm.wA.dinc(sy.dma_start(out=dst[sl], in_=src_))
        for pt in range(PT):
            sm.wA.dinc(sy.dma_start(out=b.xn[:, pt, :],
                                    in_=d.xin[pt*128:(pt+1)*128, :]))
        sm.wA.dinc(sy.dma_start(out=b.bkbc[0:1, :], in_=d.bk_row))
        sm.wA.dinc(sy.dma_start(out=b.bvbc[0:1, :], in_=d.bv_row))
        sy.wait_ge(sm.wA.h, sm.wA.n)

    @block.vector
    def _(v):
        v.memset(b.onesrow[:, :], 1.0)
        v.memset(b.ones128[:, :], 1.0)
        v.memset(a.ones16[:, :], 1.0)

    bar(block, "tvsgy")

    @block.scalar
    def _(sc):
        sc.activation(out=b.edr[:, :], in_=b.tdr[:, :], func=AF.Exp)
        sc.activation(out=b.edc[:, :], in_=b.tdc[:, :], func=AF.Exp)
        sc.drain()
        sm.tS.inc(sc.activation(out=b.wsr[:, :], in_=b.edr[:, :], func=AF.Exp,
                                scale=-256.0))

    @block.vector
    def _(v):
        v.wait_ge(sm.tS.h, sm.tS.n)
        v.tensor_scalar(out=a.pw[:, :], in0=a.iota16[:, :], scalar1=b.edc[:, 0:1],
                        scalar2=None, op0=ALU.mult)
        v.tensor_scalar(out=a.wkr[:, :], in0=a.iota16r[:, :], scalar1=b.edc[:, 0:1],
                        scalar2=None, op0=ALU.mult)
        v.tensor_scalar(out=b.jpow[:, :], in0=b.jpow[:, :], scalar1=b.edc[:, 0:1],
                        scalar2=None, op0=ALU.mult)
        sm.tV.inc(v.drain())

    @block.scalar
    def _(sc):
        sc.wait_ge(sm.tV.h, sm.tV.n)
        sc.activation(out=a.pw[:, :], in_=a.pw[:, :], func=AF.Exp, scale=-1.0)
        sc.activation(out=a.wkr[:, :], in_=a.wkr[:, :], func=AF.Exp, scale=-1.0)
        sc.activation(out=b.jpow[:, :], in_=b.jpow[:, :], func=AF.Exp, scale=-256.0)

    bar(block, "tvs")

    @block.vector
    def _(v):
        v.tensor_tensor(out=b.jpow[:, :], in0=b.jpow[:, :], in1=b.jmask[:, :],
                        op=ALU.mult)
        v.drain()
        v.tensor_tensor(out=a.coeft[:, :].rearrange("p (hh j) -> p hh j", j=8),
                        in0=b.jpow[:, :].unsqueeze(1).broadcast_to([16, 16, 8]),
                        in1=a.selhead[:, 0:1024:8].rearrange("p (hh j) -> p hh j", j=8),
                        op=ALU.mult)

    bar(block, "tv")

    @block.tensor
    def _(t):
        t.matmul(ps[0][:, 0:16], b.onesrow[0:1, :], b.edr[0:1, :], start=True, stop=True)
        t.matmul(ps[1][:, 0:16], b.onesrow[0:1, :], b.urow[0:1, :], start=True, stop=True)
        t.matmul(ps[2][:, 0:128], a.ones16[:, :], a.coeft[:, :], start=True, stop=True)
        for par in range(2):
            for nchv in range(2):
                t.matmul(ps[3 + 2*par + nchv][:, :], a.wkr[:, par*128:par*128+128],
                         a.selhead[:, nchv*512:(nchv+1)*512], start=True, stop=True)
        sm.tT.inc(t.drain())

    @block.vector
    def _(v):
        v.wait_ge(sm.tT.h, sm.tT.n)
        v.tensor_copy(b.edbc[:, :], ps[0][:, 0:16])
        v.tensor_copy(b.ubc[:, :], ps[1][:, 0:16])
        v.tensor_copy(b.coefbc[:, :], ps[2][:, 0:128])
        for par in range(2):
            for nchv in range(2):
                v.tensor_copy(b.wkbc[:, par, nchv*512:(nchv+1)*512],
                              ps[3 + 2*par + nchv][:, :])
        sm.tV.inc(v.drain())

    bar(block, "tv")

    @block.tensor
    def _(t):
        for hp in range(8):
            t.matmul(ps[hp][:, 0:256], a.selpair[:, hp*128:hp*128+128], a.pw[:, :],
                     start=True, stop=True)
        sm.tT.inc(t.drain())

    @block.vector
    def _(v):
        v.wait_ge(sm.tT.h, sm.tT.n)
        for hp in range(8):
            v.tensor_copy(b.wbbc[:, hp, :], ps[hp][:, 0:256])
        sm.tV.inc(v.drain())

    bar(block, "tv")

    @block.tensor
    def _(t):
        for chunk in range(2):
            t.matmul(ps[chunk][:, :], b.onesrow[0:1, :],
                     b.bkbc[0:1, chunk*512:(chunk+1)*512], start=True, stop=True)
            t.matmul(ps[2 + chunk][:, :], b.onesrow[0:1, :],
                     b.bvbc[0:1, chunk*512:(chunk+1)*512], start=True, stop=True)
        sm.tT.inc(t.drain())

    @block.vector
    def _(v):
        v.wait_ge(sm.tT.h, sm.tT.n)
        for chunk in range(2):
            v.tensor_copy(b.bkbc[:, chunk*512:(chunk+1)*512], ps[chunk][:, :])
            v.tensor_copy(b.bvbc[:, chunk*512:(chunk+1)*512], ps[2 + chunk][:, :])

    bar(block, "tvsgy")

    for rnd in range(4):
        hs = range(4*rnd, 4*rnd + 4)

        @block.vector
        def _(v, hs=hs):
            for h in hs:
                for pt2 in range(2):
                    i = 2*(h % 4) + pt2
                    v.tensor_scalar(out=a.wstage[:, i, :], in0=a.dqp[:, pt2, :],
                                    scalar1=b.edbc[:, h:h+1], scalar2=None, op0=ALU.mult)
            sm.tV.inc(v.drain())

        @block.scalar
        def _(sc, hs=hs):
            sc.wait_ge(sm.tV.h, sm.tV.n)
            for i in range(8):
                sc.activation(out=a.wstage[:, i, :], in_=a.wstage[:, i, :], func=AF.Exp,
                              scale=-1.0)
            sm.tS.inc(sc.drain())

        @block.vector
        def _(v, hs=hs):
            v.wait_ge(sm.tS.h, sm.tS.n)
            for h in hs:
                for pt2 in range(2):
                    i = 2*(h % 4) + pt2
                    v.tensor_tensor(out=a.wstage[:, i, :], in0=a.wstage[:, i, :],
                                    in1=a.mupper[:, pt2, :], op=ALU.mult)
            v.drain()
            for h in hs:
                for pt2 in range(2):
                    i = 2*(h % 4) + pt2
                    v.scalar_tensor_tensor(out=a.wstage[:, i, :],
                                           in0=a.id256[:, pt2, :],
                                           scalar=b.ubc[:, h:h+1],
                                           in1=a.wstage[:, i, :],
                                           op0=ALU.mult, op1=ALU.add)
            sm.tV.inc(v.drain())

        @block.gpsimd
        def _(g, hs=hs):
            g.wait_ge(sm.tV.h, sm.tV.n)
            for h in hs:
                for pt2 in range(2):
                    i = 2*(h % 4) + pt2
                    sm.tG.dinc(g.dma_start(out=s.wmt[h, pt2], in_=a.wstage[:, i, :]))
            g.wait_ge(sm.tG.h, sm.tG.n)

        bar(block, "vsg")

    ring = DmaRing(nc, "cpk", 4)

    @block.sync
    def _(sy):
        for i in range(32):
            if i >= 4:
                sy.wait_ge(sm.tT.h, sm.tT.n + i - 3)
            ring.issue(sy, i, [(a.rmb[:, i % 4, :], d.rmblk[i]),
                               (a.rub[:, i % 4, :], d.rublk[i])])

    tT0 = sm.tT.n
    tV0 = sm.tV.n

    @block.tensor
    def _(t):
        for i in range(32):
            ring.wait_upto(t, i, per=2)
            if i >= 2:
                t.wait_ge(sm.tV.h, tV0 + i - 1)
            sm.tT.inc(t.matmul(ps[i % 2][:, 0:128], a.rmb[:, i % 4, :], a.rub[:, i % 4, :],
                               start=True, stop=True))

    @block.vector
    def _(v):
        for i in range(32):
            v.wait_ge(sm.tT.h, tT0 + i + 1)
            v.tensor_copy(b.cpack[:, i, :], ps[i % 2][:, 0:128])
            sm.tV.inc(v.drain())

    bar(block, "tvsgy")

    TGS = [(0, 512), (512, 512), (1024, 512), (1536, 512), (2048, 8)]

    @block.tensor
    def _(t):
        for pt in range(PT):
            for gi, (t0, tw) in enumerate(TGS):
                t.matmul(ps[gi][0:1, 0:tw], b.ones128[:, 0:1],
                         b.xn[:, pt, t0:t0+tw], start=(pt == 0), stop=(pt == PT-1))
        sm.tT.inc(t.drain())

    @block.vector
    def _(v):
        v.wait_ge(sm.tT.h, sm.tT.n)
        for gi, (t0, tw) in enumerate(TGS):
            v.tensor_copy(a.srow[0:1, t0:t0+tw], ps[gi][0:1, 0:tw])
        sm.tV.inc(v.drain())

    tV_sum = sm.tV.n
    tS0 = sm.tS.n
    tT1 = sm.tT.n

    @block.scalar
    def _(sc):
        for pt in range(PT):
            if pt > 0:
                sc.wait_ge(sm.tT.h, tT1 + pt)
            sm.tS.inc(sc.activation(out=a.xsq[:, :], in_=b.xn[:, pt, :], func=AF.Square))

    @block.tensor
    def _(t):
        t.wait_ge(sm.tV.h, tV_sum)
        for pt in range(PT):
            t.wait_ge(sm.tS.h, tS0 + pt + 1)
            for gi, (t0, tw) in enumerate(TGS):
                mm = t.matmul(ps[gi][0:1, 0:tw], b.ones128[:, 0:1],
                              a.xsq[:, t0:t0+tw], start=(pt == 0), stop=(pt == PT-1))
            sm.tT.inc(mm)

    @block.vector
    def _(v):
        v.wait_ge(sm.tT.h, sm.tT.n)
        for gi, (t0, tw) in enumerate(TGS):
            v.tensor_copy(a.qrow[0:1, t0:t0+tw], ps[gi][0:1, 0:tw])
        v.drain()
        v.tensor_scalar(out=a.mrow[0:1, :], in0=a.srow[0:1, :], scalar1=1.0/1024.0,
                        scalar2=None, op0=ALU.mult)
        v.tensor_scalar(out=a.qrow[0:1, :], in0=a.qrow[0:1, :], scalar1=1.0/1024.0,
                        scalar2=None, op0=ALU.mult)
        v.drain()
        v.tensor_tensor(out=a.srow[0:1, :], in0=a.mrow[0:1, :], in1=a.mrow[0:1, :],
                        op=ALU.mult)
        v.drain()
        v.tensor_tensor(out=a.qrow[0:1, :], in0=a.qrow[0:1, :], in1=a.srow[0:1, :],
                        op=ALU.subtract)
        sm.tV.inc(v.drain())

    @block.vector
    def _(v):
        v.tensor_scalar(out=a.qrow[0:1, :], in0=a.qrow[0:1, :], scalar1=1e-5,
                        scalar2=None, op0=ALU.add)
        sm.tV.inc(v.drain())

    @block.scalar
    def _(sc):
        sc.wait_ge(sm.tV.h, sm.tV.n)
        sm.tS.inc(sc.activation(out=a.qrow[0:1, :], in_=a.qrow[0:1, :], func=AF.Sqrt))

    @block.vector
    def _(v):
        v.wait_ge(sm.tS.h, sm.tS.n)
        v.reciprocal(out=a.rrow[0:1, :], in_=a.qrow[0:1, :])
        sm.tV.inc(v.drain())

    @block.tensor
    def _(t):
        t.wait_ge(sm.tV.h, sm.tV.n)
        for gi, (t0, tw) in enumerate(TGS):
            t.matmul(ps[gi][:, 0:tw], b.onesrow[0:1, :], a.mrow[0:1, t0:t0+tw],
                     start=True, stop=True)
        sm.tT.inc(t.drain())

    @block.vector
    def _(v):
        v.wait_ge(sm.tT.h, sm.tT.n)
        for gi, (t0, tw) in enumerate(TGS):
            v.tensor_copy(a.mbc[:, t0:t0+tw], ps[gi][:, 0:tw])
        sm.tV.inc(v.drain())

    @block.tensor
    def _(t):
        t.wait_ge(sm.tV.h, sm.tV.n)
        for gi, (t0, tw) in enumerate(TGS):
            t.matmul(ps[gi][:, 0:tw], b.onesrow[0:1, :], a.rrow[0:1, t0:t0+tw],
                     start=True, stop=True)
        sm.tT.inc(t.drain())

    @block.vector
    def _(v):
        v.wait_ge(sm.tT.h, sm.tT.n)
        for gi, (t0, tw) in enumerate(TGS):
            v.tensor_copy(a.rbc[:, t0:t0+tw], ps[gi][:, 0:tw])
        v.drain()
        for pt in range(PT):
            v.tensor_tensor(out=b.xn[:, pt, :], in0=b.xn[:, pt, :], in1=a.mbc[:, :],
                            op=ALU.subtract)
        v.drain()
        for pt in range(PT):
            v.tensor_tensor(out=b.xn[:, pt, :], in0=b.xn[:, pt, :], in1=a.rbc[:, :],
                            op=ALU.mult)
        v.drain()
        for pt in range(PT):
            v.tensor_scalar(out=b.xn[:, pt, 2048:2056], in0=b.xn[:, pt, 2048:2056],
                            scalar1=b.prevv[:, 0:1], scalar2=None, op0=ALU.mult)
        sm.tV.inc(v.drain())

    bar(block, "tvsgy")

    tap(ct, "xn", lambda: b.xn[:, :, :])
    tap(ct, "pw", lambda: a.pw[:, :])
    tap(ct, "wkr", lambda: a.wkr[:, :])
    tap(ct, "wkbc", lambda: b.wkbc[:, :, :])
    tap(ct, "wbbc", lambda: b.wbbc[:, :, :])
    tap(ct, "cpack", lambda: b.cpack[:, :, :])
    tap(ct, "mrow", lambda: a.mrow[0:1, :])
    tap(ct, "rrow", lambda: a.rrow[0:1, :])
    tap(ct, "coefbc", lambda: b.coefbc[:, :])
    tap(ct, "bkbc", lambda: b.bkbc[:, :])
    tap(ct, "bvbc", lambda: b.bvbc[:, :])
    tap(ct, "jmask", lambda: b.jmask[:, :])
    tap(ct, "jpow", lambda: b.jpow[:, :])
    ar.pop()


def phase1(block, ct):
    nc, d, s, b, sm, ps = ct.nc, ct.d, ct.s, ct.b, ct.sm, ct.ps
    ar = ct.ar
    ar.push()
    a_ = Ctx()
    a_.U = ar.view([128, 4096])
    a_.xx = ar.view([128, 8, 256])
    a_.mix = ar.view([128, 8, 256])
    a_.ya = a_.mix
    a_.kfm = ar.view([128, 8, 256])
    a_.rfm = ar.view([128, 8, 256])
    a_.ktok = ar.view([128, 2, 1024])
    a_.vtok = ar.view([128, 2, 1024])
    a_.atp = ar.view([128, 4, 256])
    a_.wmtb = ar.view([128, 4, 256])
    a_.wtile = ar.view([128, 2, 8, 128])
    a_.wrhs = ar.view([128, 2, 512])
    ct.p1bufs = a_

    ring_t = DmaRing(nc, "p1wt", 2)
    ring_cf = DmaRing(nc, "p1cf", 2)
    ring_cb = DmaRing(nc, "p1cb", 2)
    cspl_f = [b.state[:, 0:1024], b.state[:, 1024:2048]]
    cspl_pairs = [b.state[:, 2048:3072].bitcast(mybir.dt.bfloat16),
                  b.state[:, 3072:4096].bitcast(mybir.dt.bfloat16)]

    def cspl_p(sl, hl):
        return cspl_pairs[sl][:, 1024*hl:1024*(hl+1)].rearrange(
            "p (a c) -> p a c", a=8)

    def cspl_fv(sl):
        return cspl_f[sl].rearrange("p (a c) -> p a c", a=8)

    def cspl_pool_step(g, j):
        if j >= 1:
            g.tensor_tensor(out=cspl_p((j - 1) % 2, 1)[:, :, :],
                            in0=cspl_fv((j - 1) % 2)[:, :, :],
                            in1=cspl_p((j - 1) % 2, 0)[:, :, :], op=ALU.subtract)
            g.drain()
            ring_cb.issue(g, j - 1, [(s.wctp[j - 1, :, hl], cspl_p((j - 1) % 2, hl))
                                     for hl in range(2)])
            if j + 1 < 128:
                ring_cf.issue(g, j + 1, [(cspl_fv((j + 1) % 2),
                                          d.wct_t[j + 1].transpose([1, 0, 2]))])
        ring_cf.wait_upto(g, j, per=1)
        if j >= 2:
            ring_cb.wait_upto(g, j - 2, per=2)
        g.tensor_copy(cspl_p(j % 2, 0)[:, :, :], cspl_fv(j % 2)[:, :, :])
        g.drain()
    ring_r = DmaRing(nc, "p1wr", 2)
    ring_m = DmaRing(nc, "p1wm", 2)
    ring_sp = DmaRing(nc, "p1sp", 2)

    TT_LB = (["kfm%d" % ot for ot in range(8)]
             + ["ktc%d" % j for j in range(16)]
             + ["vtc%d" % j for j in range(16)]
             + ["rfm%d" % ot for ot in range(8)]
             + sum([["at%d" % h, "y1_%d" % h, "u%d" % h] for h in range(16)], []))
    TV_LB = (["xk"]
             + ["ekfm%d" % ot for ot in range(8)]
             + ["ekt%d" % gq for gq in range(4)]
             + ["xv"]
             + ["evt%d" % gq for gq in range(4)]
             + ["xr"]
             + ["erfm%d" % ot for ot in range(8)]
             + ["eatp0"]
             + sum([["eatp%d" % h, "ey1_%d" % (h-1), "eu%d" % (h-1)]
                    for h in range(1, 16)], [])
             + ["ey1_15", "eu15"])
    tT0 = sm.tT.n
    tV0 = sm.tV.n
    NT, NV = len(TT_LB), len(TV_LB)

    def ttpos(a, lb):
        return tT0 + NT * a + TT_LB.index(lb) + 1

    def tvpos(a, lb):
        return tV0 + NV * a + TV_LB.index(lb) + 1

    for a in range(8):
        @block.sync
        def _(sy, a=a):
            for ot in range(8):
                i = a * 16 + ot
                if i >= 2:
                    prev = i - 2
                    pa, pot = prev // 16, prev % 16
                    lb = ("kfm%d" % pot) if pot < 8 else ("rfm%d" % (pot - 8))
                    sy.wait_ge(sm.tT.h, ttpos(pa, lb))
                ring_t.issue(sy, i, [(a_.wtile[:, i % 2, :, :],
                                      d.wkt_t[ot].transpose([1, 0, 2]))])
            for j in range(16):
                i = a * 32 + j
                if i >= 2:
                    prev = i - 2
                    pa, pj = prev // 32, prev % 32
                    lb = ("ktc%d" % pj) if pj < 16 else ("vtc%d" % (pj - 16))
                    sy.wait_ge(sm.tT.h, ttpos(pa, lb))
                nch, kt = j // 8, j % 8
                ring_r.issue(sy, i, [(a_.wrhs[:, i % 2, :],
                                      d.wkt_p[kt*128:(kt+1)*128, nch*512:(nch+1)*512])])
            for j in range(16):
                i = a * 32 + 16 + j
                prev = i - 2
                pa, pj = prev // 32, prev % 32
                lb = ("ktc%d" % pj) if pj < 16 else ("vtc%d" % (pj - 16))
                sy.wait_ge(sm.tT.h, ttpos(pa, lb))
                nch, kt = j // 8, j % 8
                ring_r.issue(sy, i, [(a_.wrhs[:, i % 2, :],
                                      d.wvt_p[kt*128:(kt+1)*128, nch*512:(nch+1)*512])])
            for ot in range(8):
                i = a * 16 + 8 + ot
                prev = i - 2
                pa, pot = prev // 16, prev % 16
                lb = ("kfm%d" % pot) if pot < 8 else ("rfm%d" % (pot - 8))
                sy.wait_ge(sm.tT.h, ttpos(pa, lb))
                ring_t.issue(sy, i, [(a_.wtile[:, i % 2, :, :],
                                      d.wrt_t[ot].transpose([1, 0, 2]))])

        @block.scalar
        def _(sc, a=a):
            for h in range(16):
                i = a * 16 + h
                if i >= 2:
                    prev = i - 2
                    sc.wait_ge(sm.tV.h, tvpos(prev // 16, "eatp%d" % (prev % 16)))
                ring_m.issue(sc, i, [(a_.wmtb[:, 2*(h % 2):2*(h % 2)+2, :],
                                      s.wmt[h].transpose([1, 0, 2]))])

        @block.gpsimd
        def _(g, a=a):
            if a > 0:
                g.wait_ge(sm.tV.h, tvpos(a - 1, "erfm7"))
                ring_sp.issue(g, 2 * a, [(s.rfm[a - 1], a_.rfm[:, :, :])])
                g.wait_ge(sm.tV.h, tvpos(a - 1, "eu7"))
                ring_sp.issue(g, 2 * a + 1, [(s.ya[a - 1][:, 0:4, :],
                                              a_.ya[:, 0:4, :])])
                g.wait_ge(sm.tV.h, tvpos(a - 1, "eu15"))
                ring_sp.issue(g, 2 * a + 1, [(s.ya[a - 1][:, 4:8, :],
                                              a_.ya[:, 4:8, :])])
            if a == 0:
                ring_cf.issue(g, 0, [(cspl_fv(0), d.wct_t[0].transpose([1, 0, 2]))])
                ring_cf.issue(g, 1, [(cspl_fv(1), d.wct_t[1].transpose([1, 0, 2]))])
            for j16 in range(16):
                cspl_pool_step(g, a * 16 + j16)
            if a == 7:
                g.tensor_tensor(out=cspl_p(127 % 2, 1)[:, :, :],
                                in0=cspl_fv(127 % 2)[:, :, :],
                                in1=cspl_p(127 % 2, 0)[:, :, :], op=ALU.subtract)
                g.drain()
                ring_cb.issue(g, 127, [(s.wctp[127, :, hl], cspl_p(127 % 2, hl))
                                       for hl in range(2)])

        @block.tensor
        def _(t, a=a):
            for ot in range(8):
                ring_t.wait_upto(t, a * 16 + ot)
                if ot == 0:
                    t.wait_ge(sm.tV.h, tvpos(a, "xk"))
                if ot >= 2:
                    t.wait_ge(sm.tV.h, tvpos(a, "ekfm%d" % ((ot - 2) | 1)))
                elif a > 0:
                    t.wait_ge(sm.tV.h, tvpos(a - 1, "eu%d" % (14 + ot)))
                for kt in range(8):
                    mm = t.matmul(ps[ot % 2][:, 0:256], a_.wtile[:, (a*16+ot) % 2, kt, :],
                                  a_.mix[:, kt, :], start=(kt == 0), stop=(kt == 7))
                sm.tT.inc(mm)
            for nch in range(2):
                for kt in range(8):
                    j = nch * 8 + kt
                    ring_r.wait_upto(t, a * 32 + j)
                    if kt == 0 and a > 0:
                        t.wait_ge(sm.tV.h, tvpos(a - 1, "eatp15"))
                    sl = (a * 32 + j) % 2
                    t.matmul(ps[2 + nch][:, :], a_.mix[:, kt, 0:128],
                             a_.wrhs[:, sl, :], start=(kt == 0), stop=(kt == 7))
                    mm = t.matmul(ps[4 + nch][:, :], a_.mix[:, kt, 128:256],
                                  a_.wrhs[:, sl, :], start=(kt == 0), stop=(kt == 7))
                    sm.tT.inc(mm)
            for nch in range(2):
                for kt in range(8):
                    j = nch * 8 + kt
                    ring_r.wait_upto(t, a * 32 + 16 + j)
                    if kt == 0:
                        t.wait_ge(sm.tV.h, tvpos(a, "xv"))
                        if nch == 0 and a > 0:
                            t.wait_ge(sm.tV.h, tvpos(a - 1, "eu15"))
                        if nch == 1:
                            t.wait_ge(sm.tV.h, tvpos(a, "ekfm7"))
                    bb = [6, 7] if nch == 0 else [0, 1]
                    sl = (a * 32 + 16 + j) % 2
                    t.matmul(ps[bb[0]][:, :], a_.mix[:, kt, 0:128],
                             a_.wrhs[:, sl, :], start=(kt == 0), stop=(kt == 7))
                    mm = t.matmul(ps[bb[1]][:, :], a_.mix[:, kt, 128:256],
                                  a_.wrhs[:, sl, :], start=(kt == 0), stop=(kt == 7))
                    sm.tT.inc(mm)
            for ot in range(8):
                ring_t.wait_upto(t, a * 16 + 8 + ot)
                if ot == 0:
                    t.wait_ge(sm.tV.h, tvpos(a, "xr"))
                    t.wait_ge(sm.tV.h, tvpos(a, "ekt3"))
                if ot >= 2:
                    t.wait_ge(sm.tV.h, tvpos(a, "erfm%d" % ((ot - 2) | 1)))
                for kt in range(8):
                    mm = t.matmul(ps[2 + ot % 2][:, 0:256],
                                  a_.wtile[:, (a*16+8+ot) % 2, kt, :],
                                  a_.mix[:, kt, :], start=(kt == 0), stop=(kt == 7))
                sm.tT.inc(mm)
            for h in range(16):
                hp, ho = h // 2, h % 2
                if h == 0:
                    t.wait_ge(sm.tV.h, tvpos(a, "ekt3"))
                    t.wait_ge(sm.tV.h, tvpos(a, "erfm7"))
                else:
                    t.wait_ge(sm.tV.h, tvpos(a, "eatp%d" % (h - 1)))
                for pp in range(2):
                    mm = t.matmul(ps[4 + pp][:, 0:256],
                                  a_.kfm[ho*64:(ho+1)*64, hp, pp*128:(pp+1)*128],
                                  a_.rfm[ho*64:(ho+1)*64, hp, :],
                                  start=True, stop=True, tile_position=(ho*64, 0))
                sm.tT.inc(mm)
                t.wait_ge(sm.tV.h, tvpos(a, "eatp%d" % h))
                if h >= 2:
                    t.wait_ge(sm.tV.h, tvpos(a, "eu%d" % (h - 2)))
                elif a == 0 and h < 2:
                    pass
                for pp in range(2):
                    mm = t.matmul(ps[6 + ho][ho*64:(ho+1)*64, 0:256],
                                  a_.vtok[:, pp, 64*h:64*h+64],
                                  a_.atp[:, 2*(h % 2) + pp, :],
                                  start=(pp == 0), stop=(pp == 1),
                                  tile_position=(0, ho*64))
                sm.tT.inc(mm)
                if h >= 2:
                    t.wait_ge(sm.tV.h, tvpos(a, "eu%d" % (h - 2)))
                for qt in range(2):
                    mm = t.matmul(ps[0 + ho][ho*64:(ho+1)*64, 0:64],
                                  a_.ktok[:, qt, 64*h:64*h+64],
                                  a_.vtok[:, qt, 64*h:64*h+64],
                                  start=(qt == 0), stop=(qt == 1),
                                  tile_position=(0, ho*64))
                sm.tT.inc(mm)

        @block.vector
        def _(v, a=a):
            if a > 0:
                ring_sp.wait(v, 2 * a + 1)
            for pt in range(PT):
                v.tensor_tensor(out=a_.xx[:, pt, 1:256],
                                in0=b.xn[:, pt, a*256:a*256+255],
                                in1=b.xn[:, pt, a*256+1:a*256+256], op=ALU.subtract)
            for pt in range(PT):
                v.tensor_tensor(out=a_.xx[:, pt, 0:1],
                                in0=b.xn[:, pt, 2048+a:2049+a],
                                in1=b.xn[:, pt, a*256:a*256+1], op=ALU.subtract)
            v.drain()
            for pt in range(PT):
                v.scalar_tensor_tensor(out=a_.mix[:, pt, :], in0=a_.xx[:, pt, :],
                                       scalar=b.maa[:, pt:pt+1],
                                       in1=b.xn[:, pt, a*256:(a+1)*256],
                                       op0=ALU.mult, op1=ALU.add)
            sm.tV.inc(v.drain())
            for ot in range(8):
                v.wait_ge(sm.tT.h, ttpos(a, "kfm%d" % ot))
                v.tensor_scalar(out=a_.kfm[:, ot, :], in0=ps[ot % 2][:, 0:256],
                                scalar1=b.bias[:, 8+ot:9+ot], scalar2=None, op0=ALU.add)
                if ot % 2 == 1:
                    sm.tV.inc(v.drain(), 2)
            for nch in range(2):
                v.wait_ge(sm.tT.h, ttpos(a, "ktc%d" % (nch * 8 + 7)))
                for tt in range(2):
                    bank = (2 + nch) if tt == 0 else (4 + nch)
                    dst = a_.ktok[:, tt, nch*512:(nch+1)*512]
                    v.tensor_tensor(out=dst, in0=ps[bank][:, :],
                                    in1=b.bkbc[:, nch*512:(nch+1)*512], op=ALU.add)
                v.drain()
                for tt in range(2):
                    dst = a_.ktok[:, tt, nch*512:(nch+1)*512]
                    v.tensor_tensor(out=dst, in0=dst,
                                    in1=b.wkbc[:, tt, nch*512:(nch+1)*512], op=ALU.mult)
                sm.tV.inc(v.drain(), 2)
            for pt in range(PT):
                v.scalar_tensor_tensor(out=a_.mix[:, pt, :], in0=a_.xx[:, pt, :],
                                       scalar=b.maa[:, 8+pt:9+pt],
                                       in1=b.xn[:, pt, a*256:(a+1)*256],
                                       op0=ALU.mult, op1=ALU.add)
            sm.tV.inc(v.drain())
            for nch in range(2):
                v.wait_ge(sm.tT.h, ttpos(a, "vtc%d" % (nch * 8 + 7)))
                for tt in range(2):
                    bank = ([6, 7] if nch == 0 else [0, 1])[tt]
                    v.tensor_tensor(out=a_.vtok[:, tt, nch*512:(nch+1)*512],
                                    in0=ps[bank][:, :],
                                    in1=b.bvbc[:, nch*512:(nch+1)*512], op=ALU.add)
                sm.tV.inc(v.drain(), 2)
            for pt in range(PT):
                v.scalar_tensor_tensor(out=a_.mix[:, pt, :], in0=a_.xx[:, pt, :],
                                       scalar=b.maa[:, 16+pt:17+pt],
                                       in1=b.xn[:, pt, a*256:(a+1)*256],
                                       op0=ALU.mult, op1=ALU.add)
            sm.tV.inc(v.drain())
            if a > 0:
                ring_sp.wait(v, 2 * a)
            for ot in range(8):
                v.wait_ge(sm.tT.h, ttpos(a, "rfm%d" % ot))
                v.tensor_scalar(out=a_.rfm[:, ot, :], in0=ps[2 + ot % 2][:, 0:256],
                                scalar1=b.bias[:, ot:ot+1], scalar2=None, op0=ALU.add)
                if ot % 2 == 1:
                    sm.tV.inc(v.drain(), 2)
            for h in range(16):
                hp, ho = h // 2, h % 2
                v.wait_ge(sm.tT.h, ttpos(a, "at%d" % h))
                ring_m.wait_upto(v, a * 16 + h)
                for pp in range(2):
                    v.tensor_tensor(out=a_.atp[:, 2*ho + pp, :], in0=ps[4 + pp][:, 0:256],
                                    in1=a_.wmtb[:, 2*ho + pp, :], op=ALU.mult)
                sm.tV.inc(v.drain())
                if h >= 1:
                    hh = h - 1
                    hho = hh % 2
                    v.wait_ge(sm.tT.h, ttpos(a, "y1_%d" % hh))
                    v.tensor_copy(a_.ya[hho*64:(hho+1)*64, hh // 2, :],
                                  ps[6 + hho][hho*64:(hho+1)*64, 0:256])
                    v.wait_ge(sm.tT.h, ttpos(a, "u%d" % hh))
                    ucol = ((hh // 2) * 8 + a) * 64
                    v.tensor_copy(a_.U[hho*64:(hho+1)*64, ucol:ucol+64],
                                  ps[0 + hho][hho*64:(hho+1)*64, 0:64])
                    sm.tV.inc(v.drain(), 2)
            for hh in [15]:
                hho = hh % 2
                v.wait_ge(sm.tT.h, ttpos(a, "y1_%d" % hh))
                v.tensor_copy(a_.ya[hho*64:(hho+1)*64, hh // 2, :],
                              ps[6 + hho][hho*64:(hho+1)*64, 0:256])
                v.wait_ge(sm.tT.h, ttpos(a, "u%d" % hh))
                ucol = ((hh // 2) * 8 + a) * 64
                v.tensor_copy(a_.U[hho*64:(hho+1)*64, ucol:ucol+64],
                              ps[0 + hho][hho*64:(hho+1)*64, 0:64])
                sm.tV.inc(v.drain(), 2)

    @block.gpsimd
    def _(g):
        g.wait_ge(sm.tV.h, tvpos(7, "erfm7"))
        ring_sp.issue(g, 16, [(s.rfm[7], a_.rfm[:, :, :])])
        g.wait_ge(sm.tV.h, tvpos(7, "eu15"))
        ring_sp.issue(g, 17, [(s.ya[7], a_.ya[:, :, :])])
        g.wait_ge(sm.tV.h, tvpos(7, "eu15"))
        sm.wC.dinc(g.dma_start(out=s.cc_in, in_=a_.U[:, :]))
        g.wait_ge(sm.wC.h, sm.wC.n)
        if ct.ncores > 1:
            cc = g.collective_compute(
                "AllGather", mybir.AluOpType.bypass,
                ins=[s.cc_in], outs=[s.cc_out],
                replica_groups=[list(range(ct.ncores))])
            sm.ccs.inc(cc)
        else:
            sm.ccs.dinc(g.dma_start(out=s.cc_out[0], in_=s.cc_in))
        ring_sp.wait(g, 16)
        ring_sp.wait(g, 17)
        for sl_ in range(2):
            g.wait_ge(ring_cb.sems[sl_].h, ring_cb.sems[sl_].n)
        sm.p1gate.inc(g.nop())

    ct.p1_done_ccs = sm.ccs.n
    ct.p1_done_tG = sm.p1gate.n

    tap(ct, "p1_U", lambda: a_.U[:, :])
    tap(ct, "p1_kfm", lambda: a_.kfm[:, :, :])
    tap(ct, "p1_rfm", lambda: a_.rfm[:, :, :])
    tap(ct, "p1_ktok", lambda: a_.ktok[:, :, :])
    tap(ct, "p1_vtok", lambda: a_.vtok[:, :, :])
    tap(ct, "p1_ya", lambda: a_.ya[:, :, :])
    ar.pop()


def phase2(block, ct):
    nc, d, s, b, sm, ps = ct.nc, ct.d, ct.s, ct.b, ct.sm, ct.ps
    ar = ct.ar
    ar.push()
    a_ = Ctx()
    BF16 = mybir.dt.bfloat16
    a_.zpair = ar.view([128, 4096])
    a_.zh = a_.zpair[:, 0:2048].bitcast(BF16).rearrange("p (a c) -> p a c", a=8)
    a_.zl = a_.zpair[:, 2048:4096].bitcast(BF16).rearrange("p (a c) -> p a c", a=8)
    a_.zf = ar.view([128, 2, 512])
    a_.acc = ar.view([128, 8, 512])
    a_.wct = ar.view([128, 8, 2, 8, 128], BF16)
    a_.utmp = ar.view([128, 4096])
    a_.zh2 = a_.utmp[:, 0:2048].bitcast(BF16).rearrange("p (a c) -> p a c", a=8)
    a_.zl2 = a_.utmp[:, 2048:4096].bitcast(BF16).rearrange("p (a c) -> p a c", a=8)

    def zh_(Bi):
        return a_.zh if Bi % 2 == 0 else a_.zh2

    def zl_(Bi):
        return a_.zl if Bi % 2 == 0 else a_.zl2
    sm_z = Sem(nc, "zsp")
    ct.p2bufs = a_

    ring_w = DmaRing(nc, "p2w", 1)
    ring_st = DmaRing(nc, "p2st", 2)
    ring_u = DmaRing(nc, "p2u", 2)
    ring_sf = DmaRing(nc, "p2sf", 2)
    ring_sb = DmaRing(nc, "p2sb", 2)
    spl_f = [b.state[:, 0:1024], b.state[:, 1024:2048]]
    spl_pairs = [b.state[:, 2048:3072].bitcast(mybir.dt.bfloat16),
                 b.state[:, 3072:4096].bitcast(mybir.dt.bfloat16)]

    def spl_p(sl, hl):
        return spl_pairs[sl][:, 1024*hl:1024*(hl+1)].rearrange(
            "p (a c) -> p a c", a=8)

    def spl_fv(sl):
        return spl_f[sl].rearrange("p (a c) -> p a c", a=8)

    CH = ([("l1", w_, kt, 0) for w_ in range(3) for kt in range(8)]
          + [("w1", e, ht, 0) for e in range(8) for ht in range(16)]
          + [("w2", e, ot, hf) for e in range(8) for ot in range(8)
             for hf in range(2)])
    assert len(CH) == 280

    gate_tG = ct.p1_done_tG

    NB = 32 * 2
    tT0, tS0, tV0 = sm.tT.n, sm.tS.n, sm.tV.n

    def B(tg, g, half):
        return (tg * 8 + g) * 2 + half

    TTL2 = ["r0_%d" % zi for zi in range(8)]
    for _Bi in range(NB):
        TTL2 += ["c%d_%d" % (_Bi, ot) for ot in range(4)]
        if _Bi + 1 < NB:
            TTL2 += ["r%d_%d" % (_Bi + 1, zi) for zi in range(4)]
        TTL2 += ["c%d_%d" % (_Bi, ot) for ot in range(4, 8)]
        if _Bi + 1 < NB:
            TTL2 += ["r%d_%d" % (_Bi + 1, zi) for zi in range(4, 8)]
    TTL2IDX = {lb: k for k, lb in enumerate(TTL2)}

    def ttp(Bi, idx):
        lb = ("r%d_%d" % (Bi, idx)) if idx < 8 else ("c%d_%d" % (Bi, idx - 8))
        return tT0 + TTL2IDX[lb] + 1

    def tsp(Bi, zi):
        return tS0 + 8 * Bi + zi + 1

    def tvp(Bi, ot):
        return tV0 + 8 * Bi + ot + 1

    @block.sync
    def _(sy):
        sy.wait_ge(sm.p1gate.h, gate_tG)
        for tg in range(4):
            for g in range(8):
                for half in range(2):
                    Bi = B(tg, g, half)
                    if Bi >= 1:
                        sy.wait_ge(sm.tT.h, ttp(Bi - 1, 15))
                    dmas = []
                    for h in (0, 1) if half == 0 else (2, 3):
                        zk0 = h * 32 + 4 * g
                        zi0 = 4 * (h % 2)
                        dmas.append((a_.wct[:, zi0:zi0+4, :, :, :],
                                     s.wctp[zk0:zk0+4].transpose([1, 0, 2, 3, 4])))
                    ring_w.issue(sy, Bi, dmas)
            sy.wait_ge(sm.tV.h, tvp(B(tg, 7, 1), 7))
            ring_st.issue(sy, tg, [(s.ffn0[tg], a_.acc[:, :, :])])

    def rails_mms(t, Bi, hh):
        half, g, tg = Bi % 2, (Bi // 2) % 8, Bi // 16
        tsl = slice(512 * tg, 512 * (tg + 1))
        h = half * 2 + hh
        for rt in range(4):
            zi = 4 * hh + rt
            if Bi > 0:
                t.wait_ge(sm.tS.h, tsp(Bi - 1, 4 + rt) if hh == 0
                          else tsp(Bi, rt))
            elif hh == 1:
                t.wait_ge(sm.tS.h, tsp(Bi, rt))
            mm = t.matmul(ps[rt][:, :],
                          b.cpack[32*rt:32*rt+32, 4*g + h, :],
                          b.xn[32*rt:32*rt+32, g, tsl],
                          start=True, stop=True,
                          tile_position=(32*rt, 0))
            sm.tT.inc(mm)

    @block.tensor
    def _(t):
        t.wait_ge(sm.p1gate.h, gate_tG)
        rails_mms(t, 0, 0)
        rails_mms(t, 0, 1)
        for Bi in range(NB):
            ring_w.wait_upto(t, Bi, per=2)
            for ot in range(8):
                bank = 4 + ot % 4
                if ot >= 4:
                    t.wait_ge(sm.tV.h, tvp(Bi, ot - 4))
                elif Bi > 0:
                    t.wait_ge(sm.tV.h, tvp(Bi - 1, ot + 4))
                n3 = 0
                for zi in range(8):
                    if ot == 0:
                        t.wait_ge(sm_z.h, 8 * Bi + zi + 1)
                    for (wi, xi) in [(0, 0), (0, 1), (1, 0)]:
                        n3 += 1
                        zx = zh_(Bi) if xi == 0 else zl_(Bi)
                        mm = t.matmul(ps[bank][:, :],
                                      a_.wct[:, zi, wi, ot, :],
                                      zx[:, zi, :],
                                      start=(n3 == 1), stop=(n3 == 24))
                sm.tT.inc(mm)
                if ot == 3 and Bi + 1 < NB:
                    rails_mms(t, Bi + 1, 0)
            if Bi + 1 < NB:
                rails_mms(t, Bi + 1, 1)

    @block.scalar
    def _(sc):
        sc.wait_ge(sm.p1gate.h, gate_tG)
        for tg in range(4):
            for g in range(8):
                for half in range(2):
                    Bi = B(tg, g, half)
                    for hh in range(2):
                        for rt in range(4):
                            zi = 4 * hh + rt
                            sc.wait_ge(sm.tT.h, ttp(Bi, zi))
                            if 8 * Bi + zi - 1 > 0:
                                sc.wait_ge(sm_z.h, max(8 * Bi + zi - 1, 0))
                            sm.tS.inc(sc.activation(out=a_.zf[:, zi % 2, :],
                                                    in_=ps[rt][:, :], func=AF.Gelu))

    def z_steps(v, Bi, zis):
        zh, zl = zh_(Bi), zl_(Bi)
        for zi in zis:
            v.wait_ge(sm.tS.h, tsp(Bi, zi))
            v.tensor_copy(zh[:, zi, :], a_.zf[:, zi % 2, :])
            if zi >= 1:
                v.tensor_tensor(out=zl[:, zi - 1, :],
                                in0=a_.zf[:, (zi - 1) % 2, :],
                                in1=zh[:, zi - 1, :], op=ALU.subtract)
                sm_z.inc(v.drain())
            else:
                v.drain()
        if zis[-1] == 7:
            v.tensor_tensor(out=zl[:, 7, :], in0=a_.zf[:, 7 % 2, :],
                            in1=zh[:, 7, :], op=ALU.subtract)
            sm_z.inc(v.drain())

    def evacs(v, Bi, ots):
        g, half = (Bi // 2) % 8, Bi % 2
        tg = Bi // 16
        for ot in ots:
            if tg > 0 and g == 0 and half == 0 and ot == 0:
                ring_st.wait_upto(v, tg - 1, per=1)
            v.wait_ge(sm.tT.h, ttp(Bi, 8 + ot))
            bank = 4 + ot % 4
            if g == 0 and half == 0:
                v.tensor_copy(a_.acc[:, ot, :], ps[bank][:, :])
            else:
                v.tensor_tensor(out=a_.acc[:, ot, :], in0=ps[bank][:, :],
                                in1=a_.acc[:, ot, :], op=ALU.add)
            sm.tV.inc(v.drain())

    @block.vector
    def _(v):
        for Bi in range(NB):
            if Bi > 0:
                evacs(v, Bi - 1, range(6))
            z_steps(v, Bi, range(4))
            if Bi > 0:
                evacs(v, Bi - 1, (6, 7))
            z_steps(v, Bi, range(4, 8))
        evacs(v, NB - 1, range(8))
        for sl_ in range(2):
            v.wait_ge(ring_sb.sems[sl_].h, 4480)
        v.memset(b.state[:, :], 0.0)
        sm.tV.inc(v.drain())
        v.wait_ge(sm.ccs.h, ct.p1_done_ccs)
        for j in range(ct.ncores):
            ring_u.wait_upto(v, j, per=1)
            ub = a_.zpair if j % 2 == 0 else a_.utmp
            for h in range(16):
                hp, ho = h // 2, h % 2
                sl = (slice(ho*64, ho*64+64), slice(hp*512, (hp+1)*512))
                v.scalar_tensor_tensor(out=b.state[sl[0], sl[1]],
                                       in0=ub[sl[0], sl[1]],
                                       scalar=b.coefbc[sl[0], 8*h+j:8*h+j+1],
                                       in1=b.state[sl[0], sl[1]],
                                       op0=ALU.mult, op1=ALU.add)
            sm.tV.inc(v.drain())

    def spl_src(i):
        kind, e, x2, hf = CH[i]
        if kind == "w1":
            return d.w1_t[e, x2].transpose([1, 0, 2])
        if kind == "w2":
            return d.w2_t[e, x2, 8*hf:8*hf+8].transpose([1, 0, 2])
        wsrc3 = [d.wk1_p, d.wv1_p, d.wq1_p][e]
        return wsrc3[x2*128:(x2+1)*128, :].rearrange("p (a c) -> p a c", a=8)

    def spl_dsts(i):
        pk, pe, px, phf = CH[i]
        if pk == "l1":
            return [(s.l1p[pe, px, :, hl, :].rearrange("p (a c) -> p a c", a=8),
                     spl_p(i % 2, hl)) for hl in range(2)]
        if pk == "w1":
            return [(s.w1p[pe, px, :, hl, :, :], spl_p(i % 2, hl))
                    for hl in range(2)]
        return [(s.w2p[pe, px, :, hl, 8*phf:8*phf+8], spl_p(i % 2, hl))
                for hl in range(2)]

    def spl_sub(g, c):
        g.tensor_tensor(out=spl_p(c % 2, 1)[:, :, :],
                        in0=spl_fv(c % 2)[:, :, :],
                        in1=spl_p(c % 2, 0)[:, :, :], op=ALU.subtract)
        g.drain()
        ring_sb.issue(g, c, spl_dsts(c))

    @block.gpsimd
    def _(g):
        g.wait_ge(sm.p1gate.h, gate_tG)
        ring_sf.issue(g, 0, [(spl_fv(0), spl_src(0))])
        ring_sf.issue(g, 1, [(spl_fv(1), spl_src(1))])
        for c in range(280):
            ring_sf.wait_upto(g, c, per=1)
            if c >= 2:
                ring_sb.wait_upto(g, c - 2, per=2)
            g.tensor_copy(spl_p(c % 2, 0)[:, :, :], spl_fv(c % 2)[:, :, :])
            g.drain()
            if c >= 1:
                spl_sub(g, c - 1)
                if c + 1 < 280:
                    ring_sf.issue(g, c + 1, [(spl_fv((c + 1) % 2), spl_src(c + 1))])
        spl_sub(g, 279)
        g.wait_ge(sm.ccs.h, ct.p1_done_ccs)
        for j in range(ct.ncores):
            if j == 0:
                g.wait_ge(sm.tT.h, ttp(NB - 2, 15))
            elif j == 1:
                g.wait_ge(sm.tT.h, ttp(NB - 1, 15))
            else:
                g.wait_ge(sm.tV.h, tV0 + 8 * NB + j)
            ub = a_.zpair if j % 2 == 0 else a_.utmp
            ring_u.issue(g, j, [(ub[:, :], s.cc_out[j])])

    ct.p2_state_tV = sm.tV.n
    ct.p2_gelu_tS = sm.tS.n
    ct.p2_ring_st = ring_st
    ct.p2_ring_sb = ring_sb
    tap(ct, "p2_acc", lambda: a_.acc[:, :, :])
    tap(ct, "p2_state", lambda: b.state[:, :])
    ar.pop()


def phase3(block, ct):
    pass


def phase1b(block, ct):
    nc, d, s, b, sm, ps = ct.nc, ct.d, ct.s, ct.b, ct.sm, ct.ps
    ar = ct.ar
    ar.push()
    a_ = Ctx()
    a_.yb = ar.view([128, 2, 8, 256])
    a_.rf = ar.view([128, 2, 8, 256])
    a_.wot = ar.view([128, 2, 8, 128])
    a_.stage = ar.view([128, 2, 8, 256])
    a_.tmp = ar.view([128, 2, 256])

    ring_y = DmaRing(nc, "pby", 2)
    ring_r2 = DmaRing(nc, "pbr", 2)
    ring_w2 = DmaRing(nc, "pbw", 2)
    ring_s2 = DmaRing(nc, "pbs", 2)

    state_tV = ct.p2_state_tV
    gelu_tS = ct.p2_gelu_tS
    tT0, tS0, tV0 = sm.tT.n, sm.tS.n, sm.tV.n
    TTL = ["y2_%d" % h for h in range(16)] + ["op%d" % ot for ot in range(8)]
    TVL = ["m0"] + sum([["m%d" % h, "ad%d" % (h-1)] for h in range(1, 16)], []) \
        + ["ad15"] + ["eo%d" % ot for ot in range(8)]
    NT, NV = len(TTL), len(TVL)

    def ttp(a, lb):
        return tT0 + NT * a + TTL.index(lb) + 1

    def tvp(a, lb):
        return tV0 + NV * a + TVL.index(lb) + 1

    for a in range(8):
        @block.gpsimd
        def _(g, a=a):
            if a >= 2:
                g.wait_ge(sm.tT.h, ttp(a - 2, "op7"))
                g.wait_ge(sm.tT.h, ttp(a - 2, "y2_15"))
            elif a == 0:
                g.wait_ge(sm.tV.h, state_tV)
                for sl_ in range(2):
                    g.wait_ge(ct.p2_ring_st.sems[sl_].h, ct.p2_ring_st.sems[sl_].n)
            ring_y.issue(g, a, [(a_.yb[:, a % 2, :, :], s.ya[a])])
            ring_r2.issue(g, a, [(a_.rf[:, a % 2, :, :], s.rfm[a])])

        @block.sync
        def _(sy, a=a):
            if a == 0:
                sy.wait_ge(sm.tV.h, state_tV)
                for sl_ in range(2):
                    sy.wait_ge(ct.p2_ring_st.sems[sl_].h, ct.p2_ring_st.sems[sl_].n)
            for ot in range(8):
                i = 8 * a + ot
                if i >= 2:
                    pa, pot = (i - 2) // 8, (i - 2) % 8
                    sy.wait_ge(sm.tT.h, ttp(pa, "op%d" % pot))
                ring_w2.issue(sy, i, [(a_.wot[:, i % 2, :, :],
                                       d.wot_t[ot].transpose([1, 0, 2]))])

        @block.tensor
        def _(t, a=a):
            if a == 0:
                t.wait_ge(sm.tV.h, state_tV)
                t.wait_ge(sm.tS.h, gelu_tS)
            ring_r2.wait_upto(t, a, per=1)
            for h in range(16):
                hp, ho = h // 2, h % 2
                if h >= 2:
                    t.wait_ge(sm.tV.h, tvp(a, "m%d" % (h - 2)))
                elif a > 0:
                    t.wait_ge(sm.tV.h, tvp(a - 1, "m%d" % (14 + h)))
                scol = hp * 512 + a * 64
                mm = t.matmul(ps[ho][ho*64:(ho+1)*64, 0:256],
                              b.state[ho*64:(ho+1)*64, scol:scol+64],
                              a_.rf[ho*64:(ho+1)*64, a % 2, hp, :],
                              start=True, stop=True, tile_position=(ho*64, ho*64))
                sm.tT.inc(mm)
            t.wait_ge(sm.tV.h, tvp(a, "ad15"))
            for ot in range(8):
                ring_w2.wait_upto(t, 8 * a + ot, per=1)
                if ot >= 2:
                    t.wait_ge(sm.tV.h, tvp(a, "eo%d" % (ot - 2)))
                elif a > 0:
                    t.wait_ge(sm.tV.h, tvp(a - 1, "eo%d" % (6 + ot)))
                for kt in range(8):
                    mm = t.matmul(ps[2 + ot % 2][:, 0:256],
                                  a_.wot[:, (8 * a + ot) % 2, kt, :],
                                  a_.yb[:, a % 2, kt, :],
                                  start=(kt == 0), stop=(kt == 7))
                sm.tT.inc(mm)

        @block.vector
        def _(v, a=a):
            ring_y.wait_upto(v, a, per=1)
            for h in range(16):
                hp, ho = h // 2, h % 2
                v.wait_ge(sm.tT.h, ttp(a, "y2_%d" % h))
                v.tensor_tensor(out=a_.tmp[ho*64:(ho+1)*64, h % 2, :],
                                in0=ps[ho][ho*64:(ho+1)*64, 0:256],
                                in1=b.wbbc[ho*64:(ho+1)*64, hp, :], op=ALU.mult)
                sm.tV.inc(v.drain())
                if h >= 1:
                    hh = h - 1
                    hho = hh % 2
                    sl = a_.yb[hho*64:(hho+1)*64, a % 2, hh // 2, :]
                    v.tensor_tensor(out=sl, in0=a_.tmp[hho*64:(hho+1)*64, hh % 2, :],
                                    in1=sl, op=ALU.add)
                    sm.tV.inc(v.drain())
            sl = a_.yb[64:128, a % 2, 7, :]
            v.tensor_tensor(out=sl, in0=a_.tmp[64:128, 15 % 2, :], in1=sl, op=ALU.add)
            sm.tV.inc(v.drain())
            if a >= 2:
                ring_s2.wait_upto(v, a - 2, per=1)
            for ot in range(8):
                v.wait_ge(sm.tT.h, ttp(a, "op%d" % ot))
                v.tensor_scalar(out=a_.stage[:, a % 2, ot, :], in0=ps[2 + ot % 2][:, 0:256],
                                scalar1=b.bias[:, 24+ot:25+ot], scalar2=None, op0=ALU.add)
                sm.tV.inc(v.drain())

        @block.gpsimd
        def _(g, a=a):
            g.wait_ge(sm.tV.h, tvp(a, "eo7"))
            ring_s2.issue(g, a, [(s.attn0[a], a_.stage[:, a % 2, :, :])])

    @block.gpsimd
    def _(g):
        ring_s2.wait_upto(g, 6, per=1)
        ring_s2.wait_upto(g, 7, per=1)
        sm.p1gate.inc(g.nop())

    ct.p1b_gate = sm.p1gate.n
    tap(ct, "pb_stage", lambda: a_.stage[:, :, :, :])
    tap(ct, "pb_yb", lambda: a_.yb[:, :, :, :])
    ar.pop()


def phase4(block, ct):
    nc, d, s, b, sm, ps = ct.nc, ct.d, ct.s, ct.b, ct.sm, ct.ps
    ar = ct.ar
    ar.push()
    a_ = Ctx()
    a_.xc = ar.view([128, 2, 512])
    a_.ac = ar.view([128, 2, 512])
    a_.fc = ar.view([128, 2, 512])
    a_.x1c = ar.view([128, 2, 512])
    a_.xq = ar.view([128, 2, 512])
    a_.mrow = ar.view([1, TOK])
    a_.rrow = ar.view([1, TOK])
    a_.srow = ar.view([1, TOK])
    a_.qrow = ar.view([1, TOK])
    a_.mbc = ar.view([128, TOK])
    a_.rbc = ar.view([128, TOK])
    a_.xps = ar.view([128, 2, 2, 512], mybir.dt.bfloat16)

    ring_l = DmaRing(nc, "p4l", 2)
    ring_xp = DmaRing(nc, "p4xp", 2)
    ring_x = DmaRing(nc, "p4x", 2)
    ring_st4 = DmaRing(nc, "p4s", 2)
    gate = ct.p1b_gate
    tT0, tS0, tV0 = sm.tT.n, sm.tS.n, sm.tV.n
    TGS4 = [(0, 512), (512, 512), (1024, 512), (1536, 512)]
    NCH = 32

    def ci(pt, tg):
        return pt * 4 + tg

    @block.sync
    def _(sy):
        sy.wait_ge(sm.p1gate.h, gate)
        for pt in range(PT):
            for tg in range(4):
                i = ci(pt, tg)
                if i >= 2:
                    sy.wait_ge(sm.tS.h, tS0 + i - 1)
                tsl = slice(512 * tg, 512 * (tg + 1))
                ring_l.issue(sy, i, [
                    (a_.xc[:, i % 2, :], d.xin[pt*128:(pt+1)*128, tsl]),
                    (a_.ac[:, i % 2, 0:256], s.attn0[2*tg, :, pt, :]),
                    (a_.ac[:, i % 2, 256:512], s.attn0[2*tg+1, :, pt, :]),
                    (a_.fc[:, i % 2, :], s.ffn0[tg, :, pt, :]),
                ])

    @block.vector
    def _(v):
        for pt in range(PT):
            for tg in range(4):
                i = ci(pt, tg)
                ring_l.wait_upto(v, i, per=4)
                if i >= 2:
                    ring_st4.wait_upto(v, i - 2, per=1)
                    v.wait_ge(sm.tT.h, tT0 + i - 1)
                v.tensor_tensor(out=a_.x1c[:, i % 2, :], in0=a_.xc[:, i % 2, :],
                                in1=a_.ac[:, i % 2, :], op=ALU.add)
                v.drain()
                v.tensor_tensor(out=a_.x1c[:, i % 2, :], in0=a_.x1c[:, i % 2, :],
                                in1=a_.fc[:, i % 2, :], op=ALU.add)
                sm.tV.inc(v.drain())

    @block.scalar
    def _(sc):
        for pt in range(PT):
            for tg in range(4):
                i = ci(pt, tg)
                sc.wait_ge(sm.tV.h, tV0 + i + 1)
                if i >= 1:
                    sc.wait_ge(sm.tT.h, tT0 + i)
                sm.tS.inc(sc.activation(out=a_.xq[:, i % 2, :], in_=a_.x1c[:, i % 2, :],
                                        func=AF.Square))

    @block.gpsimd
    def _(g):
        for pt in range(PT):
            for tg in range(4):
                i = ci(pt, tg)
                g.wait_ge(sm.tV.h, tV0 + i + 1)
                ring_st4.issue(g, i, [(s.x1[pt, :, 512*tg:512*(tg+1)],
                                       a_.x1c[:, i % 2, :])])

    @block.tensor
    def _(t):
        for pt in range(PT):
            for tg in range(4):
                i = ci(pt, tg)
                t.wait_ge(sm.tS.h, tS0 + i + 1)
                t.matmul(ps[tg][0:1, :], b.ones128[:, 0:1], a_.x1c[:, i % 2, :],
                         start=(pt == 0), stop=(pt == PT - 1))
                mm = t.matmul(ps[4 + tg][0:1, :], b.ones128[:, 0:1], a_.xq[:, i % 2, :],
                              start=(pt == 0), stop=(pt == PT - 1))
                sm.tT.inc(mm)

    @block.vector
    def _(v):
        v.wait_ge(sm.tT.h, tT0 + NCH)
        for gi, (t0, tw) in enumerate(TGS4):
            v.tensor_copy(a_.srow[0:1, t0:t0+tw], ps[gi][0:1, :])
            v.tensor_copy(a_.qrow[0:1, t0:t0+tw], ps[4 + gi][0:1, :])
        v.drain()
        v.tensor_scalar(out=a_.mrow[0:1, :], in0=a_.srow[0:1, :], scalar1=1.0/1024.0,
                        scalar2=None, op0=ALU.mult)
        v.tensor_scalar(out=a_.qrow[0:1, :], in0=a_.qrow[0:1, :], scalar1=1.0/1024.0,
                        scalar2=None, op0=ALU.mult)
        v.drain()
        v.tensor_tensor(out=a_.srow[0:1, :], in0=a_.mrow[0:1, :], in1=a_.mrow[0:1, :],
                        op=ALU.mult)
        v.drain()
        v.tensor_tensor(out=a_.qrow[0:1, :], in0=a_.qrow[0:1, :], in1=a_.srow[0:1, :],
                        op=ALU.subtract)
        v.drain()
        v.tensor_scalar(out=a_.qrow[0:1, :], in0=a_.qrow[0:1, :], scalar1=1e-5,
                        scalar2=None, op0=ALU.add)
        sm.tV.inc(v.drain())

    @block.scalar
    def _(sc):
        sc.wait_ge(sm.tV.h, sm.tV.n)
        sm.tS.inc(sc.activation(out=a_.qrow[0:1, :], in_=a_.qrow[0:1, :], func=AF.Sqrt))

    @block.vector
    def _(v):
        v.wait_ge(sm.tS.h, sm.tS.n)
        v.reciprocal(out=a_.rrow[0:1, :], in_=a_.qrow[0:1, :])
        sm.tV.inc(v.drain())

    @block.tensor
    def _(t):
        t.wait_ge(sm.tV.h, sm.tV.n)
        for gi, (t0, tw) in enumerate(TGS4):
            t.matmul(ps[gi][:, :], b.onesrow[0:1, :], a_.mrow[0:1, t0:t0+tw],
                     start=True, stop=True)
            t.matmul(ps[4 + gi][:, :], b.onesrow[0:1, :], a_.rrow[0:1, t0:t0+tw],
                     start=True, stop=True)
        sm.tT.inc(t.drain())

    @block.vector
    def _(v):
        v.wait_ge(sm.tT.h, sm.tT.n)
        for gi, (t0, tw) in enumerate(TGS4):
            v.tensor_copy(a_.mbc[:, t0:t0+tw], ps[gi][:, :])
            v.tensor_copy(a_.rbc[:, t0:t0+tw], ps[4 + gi][:, :])
        sm.tV.inc(v.drain())

    tV1 = sm.tV.n

    @block.sync
    def _(sy):
        sy.wait_ge(sm.tV.h, tV1)
        for sl_ in range(2):
            sy.wait_ge(ring_st4.sems[sl_].h, ring_st4.sems[sl_].n)
        for pt in range(PT):
            for tg in range(4):
                i = ci(pt, tg)
                if i >= 2:
                    sy.wait_ge(sm.tV.h, tV1 + i - 1)
                ring_x.issue(sy, i, [(a_.xc[:, i % 2, :],
                                      s.x1[pt, :, 512*tg:512*(tg+1)])])

    @block.vector
    def _(v):
        for pt in range(PT):
            for tg in range(4):
                i = ci(pt, tg)
                ring_x.wait_upto(v, i, per=1)
                if i >= 2:
                    ring_xp.wait_upto(v, i - 2, per=2)
                tsl = slice(512 * tg, 512 * (tg + 1))
                v.tensor_tensor(out=b.xn[:, pt, tsl], in0=a_.xc[:, i % 2, :],
                                in1=a_.mbc[:, tsl], op=ALU.subtract)
                v.drain()
                v.tensor_tensor(out=b.xn[:, pt, tsl], in0=b.xn[:, pt, tsl],
                                in1=a_.rbc[:, tsl], op=ALU.mult)
                v.drain()
                v.tensor_copy(a_.xps[:, i % 2, 0, :], b.xn[:, pt, tsl])
                v.drain()
                v.tensor_tensor(out=a_.xps[:, i % 2, 1, :], in0=b.xn[:, pt, tsl],
                                in1=a_.xps[:, i % 2, 0, :], op=ALU.subtract)
                sm.tV.inc(v.drain())

    @block.gpsimd
    def _(g):
        tVb = sm.tV.n - 32
        for pt in range(PT):
            for tg in range(4):
                i = ci(pt, tg)
                g.wait_ge(sm.tV.h, tVb + i + 1)
                tsl = slice(512 * tg, 512 * (tg + 1))
                ring_xp.issue(g, i, [(s.xn1p[pt, 0, :, tsl], a_.xps[:, i % 2, 0, :]),
                                     (s.xn1p[pt, 1, :, tsl], a_.xps[:, i % 2, 1, :])])

    ct.p4_ring_xp = ring_xp
    ct.p4_xn1_tV = sm.tV.n
    tap(ct, "p4_mrow", lambda: a_.mrow[0:1, :])
    tap(ct, "p4_rrow", lambda: a_.rrow[0:1, :])
    ar.pop()


def phase5(block, ct):
    nc, d, s, b, sm, ps = ct.nc, ct.d, ct.s, ct.b, ct.sm, ct.ps
    ar = ct.ar
    ar.push()
    a_ = Ctx()
    a_.S = ar.view([128, 8192])
    a_.kvq = ar.view([128, 3, 1024])
    a_.prod = ar.view([128, 4096])
    a_.kn = ar.view([128, 1024])
    a_.tmp1 = ar.view([128, 1024])
    a_.sl = a_.prod
    a_.nrm = ar.view([128, 128])
    a_.rcp = ar.view([128, 128])
    a_.stage = a_.prod[:, 3072:4096].rearrange("p (a b) -> p a b", b=128)
    a_.wch = ar.view([128, 4, 2, 512], mybir.dt.bfloat16)
    a_.dlt = a_.kvq[:, 1, :]
    a_.xt5 = ar.view([128, 2, 2, 8, 128], mybir.dt.bfloat16)

    ring_x5 = DmaRing(nc, "p5x", 2)
    ring_w5 = DmaRing(nc, "p5w", 4)
    ring_s5 = DmaRing(nc, "p5s", 2)
    gate_tV = ct.p4_xn1_tV
    tT0, tS0, tV0 = sm.tT.n, sm.tS.n, sm.tV.n
    TTL = ["c%d" % i for i in range(48)] + ["tr%d" % i for i in range(8)]
    TVL = (["ek%d" % i for i in range(6)]
           + ["sv1", "sv2a", "sv2", "sv3", "sv4"]
           + ["etr%d" % i for i in range(8)])
    TSL = ["ss1", "ss2a", "ss2", "ss3"]
    NT, NV, NS = len(TTL), len(TVL), len(TSL)
    WP = [None, None, None]

    def U(bt, t):
        return bt * 8 + t

    def ttp(u, lb):
        return tT0 + NT * u + TTL.index(lb) + 1

    def tvp(u, lb):
        return tV0 + NV * u + TVL.index(lb) + 1

    def tsp(u, lb):
        return tS0 + NS * u + TSL.index(lb) + 1

    wsrc = [d.wk1_p, d.wv1_p, d.wq1_p]

    @block.sync
    def _(sy):
        sy.wait_ge(sm.tV.h, gate_tV)
        for sl_ in range(2):
            sy.wait_ge(ct.p4_ring_xp.sems[sl_].h, ct.p4_ring_xp.sems[sl_].n)
            sy.wait_ge(ct.p2_ring_sb.sems[sl_].h, ct.p2_ring_sb.sems[sl_].n)
        for bt in range(2):
            for t in range(8):
                u = U(bt, t)
                tok = slice(t * 256 + bt * 128, t * 256 + bt * 128 + 128)
                if u >= 2:
                    sy.wait_ge(sm.tT.h, ttp(u - 2, "c47"))
                ring_x5.issue(sy, u, [(a_.xt5[:, u % 2, :, kt, :],
                                       s.xn1p[kt, :, :, tok].transpose([1, 0, 2]))
                                      for kt in range(8)])
                for p in range(3):
                    for nch in range(2):
                        for kt in range(8):
                            i = (u * 48) + p * 16 + nch * 8 + kt
                            if i >= 4:
                                pi = i - 4
                                pu, pr = pi // 48, pi % 48
                                sy.wait_ge(sm.tT.h, ttp(pu, "c%d" % pr))
                            ring_w5.issue(sy, i, [
                                (a_.wch[:, i % 4, :, :],
                                 s.l1p[p, kt, :, :, nch*512:(nch+1)*512])])

    @block.tensor
    def _(t_):
        t_.wait_ge(sm.tV.h, gate_tV)
        for bt in range(2):
            for t in range(8):
                u = U(bt, t)
                tok = slice(t * 256 + bt * 128, t * 256 + bt * 128 + 128)
                ring_x5.wait_upto(t_, u, per=8)
                for p in range(3):
                    for nch in range(2):
                        gidx = p * 2 + nch
                        bank = gidx % 6
                        if u > 0:
                            t_.wait_ge(sm.tV.h, tvp(u - 1, "ek%d" % (gidx | 1)))
                        for kt in range(8):
                            i = (u * 48) + p * 16 + nch * 8 + kt
                            ring_w5.wait_upto(t_, i, per=1)
                            n3 = 0
                            for (wi, xi) in [(0, 0), (0, 1), (1, 0)]:
                                n3 += 1
                                mm = t_.matmul(ps[bank][:, :],
                                               a_.xt5[:, u % 2, xi, kt, :],
                                               a_.wch[:, i % 4, wi, :],
                                               start=(kt == 0 and n3 == 1),
                                               stop=(kt == 7 and n3 == 3))
                            sm.tT.inc(mm)
                t_.wait_ge(sm.tV.h, tvp(u, "sv4"))
                for pt in range(8):
                    bank = 6 + pt % 2
                    if pt >= 2:
                        t_.wait_ge(sm.tV.h, tvp(u, "etr%d" % ((pt - 2) | 1)))
                    elif u > 0:
                        t_.wait_ge(sm.tV.h, tvp(u - 1, "etr%d" % ((6 + pt) | 1)))
                    mm = t_.transpose(ps[bank][:, 0:128],
                                      a_.dlt[:, 128*pt:128*(pt+1)], b.id128[:, :])
                    sm.tT.inc(mm)

    def Schunk(hf, q):
        return a_.S[:, 4096*hf + 1024*q: 4096*hf + 1024*(q+1)].rearrange(
            "p (g i j) -> p g i j", i=8, j=8)

    def pchunk(q):
        return a_.prod[:, 1024*q:1024*(q+1)].rearrange(
            "p (g i j) -> p g i j", i=8, j=8)

    def vchunk(vec, hf, q):
        return vec[:, 512*hf + 128*q: 512*hf + 128*(q+1)]

    def knb_c(hf, q):
        return (vchunk(a_.kn, hf, q).rearrange("p (g j) -> p g j", j=8)
                .unsqueeze(2).broadcast_to([128, 16, 8, 8]))

    @block.vector
    def _(v):
        for sl_ in range(2):
            v.wait_ge(ct.p4_ring_xp.sems[sl_].h, ct.p4_ring_xp.sems[sl_].n)
        for bt in range(2):
            for t in range(8):
                u = U(bt, t)
                if t == 0:
                    v.memset(a_.S[:, :], 0.0)
                    v.drain()
                for p in range(3):
                    for nch in range(2):
                        gidx = p * 2 + nch
                        v.wait_ge(sm.tT.h, ttp(u, "c%d" % (gidx * 8 + 7)))
                        v.tensor_copy(a_.kvq[:, p, nch*512:(nch+1)*512],
                                      ps[gidx % 6][:, :])
                        if nch == 1:
                            sm.tV.inc(v.drain(), 2)
                k, vv, q = a_.kvq[:, 0, :], a_.kvq[:, 1, :], a_.kvq[:, 2, :]
                if u >= 1:
                    ring_s5.wait_upto(v, u - 1, per=1)
                for c4 in range(4):
                    v.tensor_tensor(out=a_.tmp1[:, 256*c4:256*(c4+1)],
                                    in0=k[:, 256*c4:256*(c4+1)],
                                    in1=k[:, 256*c4:256*(c4+1)], op=ALU.mult)
                v.drain()
                for c4 in range(4):
                    v.tensor_reduce(out=a_.nrm[:, 32*c4:32*(c4+1)],
                                    in_=a_.tmp1[:, 256*c4:256*(c4+1)].rearrange(
                                        "p (g j) -> p g j", j=8),
                                    axis=AX.X, op=ALU.add)
                sm.tV.inc(v.drain())
                v.wait_ge(sm.tS.h, tsp(u, "ss1"))
                v.tensor_scalar(out=a_.nrm[:, :], in0=a_.nrm[:, :], scalar1=1e-6,
                                scalar2=None, op0=ALU.add)
                v.drain()
                v.reciprocal(out=a_.rcp[:, :], in_=a_.nrm[:, :])
                v.drain()
                for c4 in range(4):
                    v.tensor_tensor(
                        out=a_.kn[:, 256*c4:256*(c4+1)].rearrange(
                            "p (g j) -> p g j", j=8),
                        in0=k[:, 256*c4:256*(c4+1)].rearrange("p (g j) -> p g j", j=8),
                        in1=a_.rcp[:, 32*c4:32*(c4+1)].unsqueeze(2)
                            .broadcast_to([128, 32, 8]),
                        op=ALU.mult)
                v.drain()
                for hf in range(2):
                    for q2 in range(4):
                        v.tensor_tensor(out=pchunk(q2), in0=Schunk(hf, q2),
                                        in1=knb_c(hf, q2), op=ALU.mult)
                    v.drain()
                    for q2 in range(4):
                        v.tensor_reduce(out=vchunk(a_.tmp1, hf, q2),
                                        in_=a_.prod[:, 1024*q2:1024*(q2+1)].rearrange(
                                            "p (gi j) -> p gi j", j=8),
                                        axis=AX.X, op=ALU.add)
                    v.drain()
                    v.engine_nop()
                for c4 in range(4):
                    v.tensor_tensor(out=a_.dlt[:, 256*c4:256*(c4+1)],
                                    in0=vv[:, 256*c4:256*(c4+1)],
                                    in1=a_.tmp1[:, 256*c4:256*(c4+1)],
                                    op=ALU.subtract)
                v.drain()
                for hf in range(2):
                    for q2 in range(4):
                        dltb = (vchunk(a_.dlt, hf, q2)
                                .rearrange("p (g i) -> p g i", i=8)
                                .unsqueeze(3).broadcast_to([128, 16, 8, 8]))
                        v.tensor_tensor(out=pchunk(q2), in0=dltb,
                                        in1=knb_c(hf, q2), op=ALU.mult)
                    v.drain()
                    for q2 in range(4):
                        hsl = slice(4096*hf + 1024*q2, 4096*hf + 1024*(q2+1))
                        v.tensor_tensor(out=a_.S[:, hsl], in0=a_.S[:, hsl],
                                        in1=a_.prod[:, 1024*q2:1024*(q2+1)],
                                        op=ALU.add)
                    sm.tV.inc(v.drain())
                for hf in range(2):
                    v.wait_ge(sm.tS.h, tsp(u, "ss2a" if hf == 0 else "ss2"))
                    for q2 in range(4):
                        qb = (vchunk(q, hf, q2).rearrange("p (g j) -> p g j", j=8)
                              .unsqueeze(2).broadcast_to([128, 16, 8, 8]))
                        v.tensor_tensor(out=pchunk(q2), in0=Schunk(hf, q2),
                                        in1=qb, op=ALU.mult)
                    v.drain()
                    for q2 in range(4):
                        v.tensor_reduce(out=vchunk(a_.tmp1, hf, q2),
                                        in_=a_.prod[:, 1024*q2:1024*(q2+1)].rearrange(
                                            "p (gi j) -> p gi j", j=8),
                                        axis=AX.X, op=ALU.add)
                    v.drain()
                sm.tV.inc(v.drain())
                v.wait_ge(sm.tS.h, tsp(u, "ss3"))
                v.tensor_tensor(out=a_.dlt[:, :], in0=a_.tmp1[:, :],
                                in1=a_.sl[:, 0:1024], op=ALU.mult)
                sm.tV.inc(v.drain())
                for pt in range(8):
                    v.wait_ge(sm.tT.h, ttp(u, "tr%d" % pt))
                    v.tensor_copy(a_.stage[:, pt, :], ps[6 + pt % 2][:, 0:128])
                    if pt % 2 == 1:
                        sm.tV.inc(v.drain(), 2)

    @block.scalar
    def _(sc):
        for bt in range(2):
            for t in range(8):
                u = U(bt, t)
                sc.wait_ge(sm.tV.h, tvp(u, "sv1"))
                sm.tS.inc(sc.activation(out=a_.nrm[:, :], in_=a_.nrm[:, :],
                                        func=AF.Sqrt))
                sc.wait_ge(sm.tV.h, tvp(u, "sv2a"))
                sm.tS.inc(sc.activation(out=a_.S[:, 0:4096], in_=a_.S[:, 0:4096],
                                        func=AF.Tanh))
                sc.wait_ge(sm.tV.h, tvp(u, "sv2"))
                sm.tS.inc(sc.activation(out=a_.S[:, 4096:8192], in_=a_.S[:, 4096:8192],
                                        func=AF.Tanh))
                sc.wait_ge(sm.tV.h, tvp(u, "sv3"))
                sm.tS.inc(sc.activation(out=a_.sl[:, 0:1024], in_=a_.tmp1[:, :],
                                        func=AF.Silu))

    @block.gpsimd
    def _(g):
        for bt in range(2):
            for t in range(8):
                u = U(bt, t)
                g.wait_ge(sm.tV.h, tvp(u, "etr7"))
                ring_s5.issue(g, u, [(s.attn1[2*t + bt], a_.stage[:, :, :])])

    @block.gpsimd
    def _(g):
        ring_s5.wait_upto(g, 14, per=1)
        ring_s5.wait_upto(g, 15, per=1)
        sm.p1gate.inc(g.nop())

    ct.p5_gate = sm.p1gate.n
    tap(ct, "p5_S", lambda: a_.S[:, :])
    tap(ct, "p5_stage", lambda: a_.stage[:, :, :])
    ar.pop()


def phase6(block, ct):
    nc, d, s, b, sm, ps = ct.nc, ct.d, ct.s, ct.b, ct.sm, ct.ps
    ar = ct.ar
    ar.push()
    a_ = Ctx()
    BF16 = mybir.dt.bfloat16
    a_.hsh = ar.view([128, 16, 512], BF16)
    a_.hsl = ar.view([128, 16, 512], BF16)
    a_.hstage = ar.view([128, 2, 512])
    a_.acc = b.state[:, :].rearrange("p (a c) -> p a c", a=8)
    a_.scob = ar.view([128, 512])
    a_.sct = ar.view([128, 16, 8])
    a_.scf = ar.view([8, 2048])
    a_.w1b = ar.view([128, 2, 2, 8, 128], BF16)
    a_.w2b = ar.view([128, 4, 2, 8, 128], BF16)
    a_.xp = b.cpack[:, :, :].rearrange("p a c -> p (a c)").bitcast(BF16)\
        .rearrange("p (h k c) -> p h k c", h=2, k=8)
    a_.wgb = b.wkbc[:, 0, 0:64].rearrange("p (a c) -> p a c", a=8)
    a_.b2b = b.wkbc[0:8, 1, :]
    a_.b1b = b.wbbc[:, 0, 0:128].rearrange("p (a c) -> p a c", a=8)
    a_.selb = b.wbbc[0:8, :, :].rearrange("p a c -> p (a c)")[0:8, 1024:2048]
    a_.xc = ar.view([128, 2, 512])
    a_.ac = ar.view([128, 2, 512])
    a_.rt1 = ar.view([128, 1])
    a_.rt2 = ar.view([128, 1])
    a_.rte = ar.view([128, 1])
    a_.rtm = ar.view([128, 16, 8])

    outd0 = sm.outd.n
    ring_w1 = DmaRing(nc, "p6w1", 2)
    ring_w2 = DmaRing(nc, "p6w2", 4)
    ring_m = DmaRing(nc, "p6m", 2)
    ring_o = DmaRing(nc, "p6o", 2)
    ring_x6 = DmaRing(nc, "p6x", 1)
    gate = ct.p5_gate
    tT0, tS0, tV0 = sm.tT.n, sm.tS.n, sm.tV.n

    @block.sync
    def _(sy):
        sy.wait_ge(sm.p1gate.h, gate)
        for kt in range(8):
            sm.wD.dinc(sy.dma_start(out=a_.wgb[:, kt, :], in_=d.wg_t[kt]))
        sm.wD.dinc(sy.dma_start(out=a_.b2b[:, :], in_=d.b2_p))
        for e in range(8):
            sm.wD.dinc(sy.dma_start(out=a_.b1b[:, e, :], in_=d.b1_col[e]))
        sm.wD.dinc(sy.dma_start(out=a_.selb[:, :], in_=d.sel8e))
    wC_small = sm.wD.n

    TT1 = (["lg0", "lg1"]
           + sum([["lg%d" % t, "st%d" % (t - 2)] for t in range(2, 16)], [])
           + ["st14", "st15"])

    def t1p(lb):
        return tT0 + TT1.index(lb) + 1

    TV1 = (["rta0", "rt0", "rta1", "rt1"]
           + sum([["rta%d" % t, "rt%d" % t, "es%d" % (t - 2)]
                  for t in range(2, 16)], [])
           + ["es14", "es15"])

    def v1p(lb):
        return tV0 + TV1.index(lb) + 1

    @block.tensor
    def _(t):
        t.wait_ge(sm.wD.h, wC_small)
        t.wait_ge(sm.p1gate.h, gate)
        for tt in range(16):
            if tt >= 2:
                t.wait_ge(sm.tV.h, v1p("rt%d" % (tt - 2)))
            for kt in range(8):
                mm = t.matmul(ps[tt % 2][:, 0:8], b.xn[:, kt, 128*tt:128*(tt+1)],
                              a_.wgb[:, kt, :], start=(kt == 0), stop=(kt == 7))
            sm.tT.inc(mm)
            if tt >= 2:
                pt2 = tt - 2
                t.wait_ge(sm.tV.h, v1p("rt%d" % pt2))
                if pt2 >= 2:
                    t.wait_ge(sm.tV.h, v1p("es%d" % (pt2 - 2)))
                sm.tT.inc(t.transpose(ps[2 + pt2 % 2][0:8, 0:128],
                                      a_.sct[:, pt2, :], b.id128[:, :]))
        for pt2 in [14, 15]:
            t.wait_ge(sm.tV.h, v1p("rt%d" % pt2))
            t.wait_ge(sm.tV.h, v1p("es%d" % (pt2 - 2)))
            sm.tT.inc(t.transpose(ps[2 + pt2 % 2][0:8, 0:128],
                                  a_.sct[:, pt2, :], b.id128[:, :]))

    @block.vector
    def _(v):
        for tt in range(16):
            v.wait_ge(sm.tT.h, t1p("lg%d" % tt))
            lg = ps[tt % 2][:, 0:8]
            v.tensor_reduce(out=a_.rt1[:, :], in_=lg, axis=AX.X, op=ALU.max)
            v.drain()
            v.tensor_scalar(out=a_.rtm[:, tt, :], in0=lg, scalar1=a_.rt1[:, 0:1],
                            scalar2=None, op0=ALU.is_equal)
            v.drain()
            v.scalar_tensor_tensor(out=a_.sct[:, tt, :], in0=a_.rtm[:, tt, :],
                                   scalar=-1e38, in1=lg, op0=ALU.mult, op1=ALU.add)
            v.drain()
            v.tensor_reduce(out=a_.rt2[:, :], in_=a_.sct[:, tt, :], axis=AX.X,
                            op=ALU.max)
            v.drain()
            v.tensor_tensor(out=a_.rte[:, :], in0=a_.rt2[:, :], in1=a_.rt1[:, :],
                            op=ALU.subtract)
            sm.tV.inc(v.drain())
            v.wait_ge(sm.tS.h, tS0 + tt + 1)
            v.tensor_scalar(out=a_.rt1[:, :], in0=a_.rte[:, :], scalar1=1.0,
                            scalar2=None, op0=ALU.add)
            v.drain()
            v.reciprocal(out=a_.rt1[:, :], in_=a_.rt1[:, 0:1])
            v.drain()
            v.tensor_tensor(out=a_.rt2[:, :], in0=a_.rte[:, :], in1=a_.rt1[:, :],
                            op=ALU.mult)
            v.drain()
            v.tensor_reduce(out=a_.rte[:, :], in_=a_.sct[:, tt, :], axis=AX.X,
                            op=ALU.max)
            v.drain()
            v.tensor_scalar(out=a_.sct[:, tt, :], in0=a_.sct[:, tt, :],
                            scalar1=a_.rte[:, 0:1], scalar2=None, op0=ALU.is_equal)
            v.drain()
            v.tensor_scalar(out=a_.rtm[:, tt, :], in0=a_.rtm[:, tt, :],
                            scalar1=a_.rt1[:, 0:1], scalar2=None, op0=ALU.mult)
            v.drain()
            v.tensor_scalar(out=a_.sct[:, tt, :], in0=a_.sct[:, tt, :],
                            scalar1=a_.rt2[:, 0:1], scalar2=None, op0=ALU.mult)
            v.drain()
            v.tensor_tensor(out=a_.sct[:, tt, :], in0=a_.sct[:, tt, :],
                            in1=a_.rtm[:, tt, :], op=ALU.add)
            sm.tV.inc(v.drain())
            if tt >= 2:
                p2_ = tt - 2
                v.wait_ge(sm.tT.h, t1p("st%d" % p2_))
                v.tensor_copy(a_.scf[:, 128*p2_:128*(p2_+1)], ps[2 + p2_ % 2][0:8, 0:128])
                sm.tV.inc(v.drain())
        for p2_ in [14, 15]:
            v.wait_ge(sm.tT.h, t1p("st%d" % p2_))
            v.tensor_copy(a_.scf[:, 128*p2_:128*(p2_+1)], ps[2 + p2_ % 2][0:8, 0:128])
            sm.tV.inc(v.drain())

    @block.scalar
    def _(sc):
        for tt in range(16):
            sc.wait_ge(sm.tV.h, v1p("rta%d" % tt))
            sm.tS.inc(sc.activation(out=a_.rte[:, :], in_=a_.rte[:, :], func=AF.Exp))

    tT1 = sm.tT.n
    tS1 = sm.tS.n
    tV1 = sm.tV.n
    TT2 = (["b2_%d" % i for i in range(8)]
           + sum([["sb%d" % e] + ["w1_%d_%d" % (e, ht) for ht in range(16)]
                  + sum([["w2a_%d_%d" % (e, ot), "w2_%d_%d" % (e, ot)]
                         for ot in range(8)], []) for e in range(8)], []))
    NT2 = len(TT2)
    TV2 = (["eb%d" % i for i in range(8)]
           + sum([["esb%d" % e, "eha%d_0" % e]
                  + sum([["eha%d_%d" % (e, h), "ehb%d_%d" % (e, h - 1)]
                         for h in range(1, 16)], [])
                  + ["ehb%d_15" % e]
                  + ["ea%d_%d" % (e, ot) for ot in range(8)] for e in range(8)], [])
           + ["mg%d" % i for i in range(8)])
    NV2 = len(TV2)
    TS2 = sum([["rl%d_%d" % (e, ht) for ht in range(16)] for e in range(8)], [])
    NS2 = len(TS2)

    def t2p(tg, lb):
        return tT1 + NT2 * tg + TT2.index(lb) + 1

    def v2p(tg, lb):
        return tV1 + NV2 * tg + TV2.index(lb) + 1

    def s2p(tg, lb):
        return tS1 + NS2 * tg + TS2.index(lb) + 1

    @block.sync
    def _(sy):
        for sl_ in range(2):
            sy.wait_ge(ct.p2_ring_sb.sems[sl_].h, ct.p2_ring_sb.sems[sl_].n)
            sy.wait_ge(ct.p4_ring_xp.sems[sl_].h, ct.p4_ring_xp.sems[sl_].n)
        for tg in range(4):
            tsl = slice(512 * tg, 512 * (tg + 1))
            if tg > 0:
                sy.wait_ge(sm.tT.h, t2p(tg - 1, "w1_7_15"))
            ring_x6.issue(sy, tg, [(a_.xp[:, hl, kt, :], s.xn1p[kt, hl, :, tsl])
                                   for kt in range(8) for hl in range(2)])
            for e in range(8):
                for ht in range(16):
                    i = (tg * 8 + e) * 16 + ht
                    if i >= 2:
                        pi = i - 2
                        pe, pht = (pi // 16) % 8, pi % 16
                        ptg = pi // 128
                        sy.wait_ge(sm.tT.h, t2p(ptg, "w1_%d_%d" % (pe, pht)))
                    ring_w1.issue(sy, i, [(a_.w1b[:, i % 2, :, :, :], s.w1p[e, ht])])
            for pt in range(8):
                i2 = tg * 8 + pt
                if i2 >= 2:
                    sy.wait_ge(sm.tV.h, v2p((i2 - 2) // 8, "mg%d" % ((i2 - 2) % 8)))
                    ring_o.wait_upto(sy, i2 - 2, per=1)
                dm = [(a_.xc[:, i2 % 2, :], s.x1[pt, :, 512*tg:512*(tg+1)])]
                at = []
                for q4 in range(4):
                    at.append((a_.ac[:, i2 % 2, 128*q4:128*(q4+1)],
                               s.attn1[4*tg + q4, :, pt, :]))
                ring_m.issue(sy, i2, dm + at)

    @block.gpsimd
    def _(g):
        for tg in range(4):
            for e in range(8):
                for ot in range(8):
                    gi = (tg * 8 + e) * 8 + ot
                    for h2 in range(2):
                        j = 2 * gi + h2
                        if j >= 4:
                            pj = j - 4
                            pgi, ph2 = pj // 2, pj % 2
                            pot2 = pgi % 8
                            pe2 = (pgi // 8) % 8
                            ptg2 = pgi // 64
                            lb = ("w2a_%d_%d" if ph2 == 0 else "w2_%d_%d") % (pe2, pot2)
                            g.wait_ge(sm.tT.h, t2p(ptg2, lb))
                        ring_w2.issue(g, j, [(a_.w2b[:, j % 4, :, :, :],
                                              s.w2p[e, ot, :, :, 8*h2:8*h2+8, :])])
            for pt in range(8):
                i2 = tg * 8 + pt
                g.wait_ge(sm.tV.h, v2p(tg, "mg%d" % pt))
                ring_o.issue(g, i2, [(ct.out[pt*128:(pt+1)*128, 512*tg:512*(tg+1)],
                                      a_.xc[:, i2 % 2, :])])
        for sl_ in range(2):
            g.wait_ge(ring_o.sems[sl_].h, ring_o.sems[sl_].n)

    @block.tensor
    def _(t):
        for tg in range(4):
            tsl = slice(512 * tg, 512 * (tg + 1))
            t.wait_ge(sm.tV.h, v1p("es15"))
            for ot in range(8):
                if ot >= 4:
                    t.wait_ge(sm.tV.h, v2p(tg, "eb%d" % (ot - 4)))
                elif tg > 0:
                    t.wait_ge(sm.tV.h, v2p(tg - 1, "ea7_%d" % (ot + 4)))
                mm = t.matmul(ps[4 + ot % 4][:, :], a_.b2b[:, ot*128:(ot+1)*128],
                              a_.scf[:, tsl], start=True, stop=True)
                sm.tT.inc(mm)
            for e in range(8):
                if tg > 0 or e > 0:
                    pe, ptg = (e - 1, tg) if e > 0 else (7, tg - 1)
                    t.wait_ge(sm.tV.h, v2p(ptg, "esb%d" % pe))
                    t.wait_ge(sm.tV.h, v2p(ptg, "eha%d_14" % pe))
                mm = t.matmul(ps[2][:, :], a_.selb[:, e*128:(e+1)*128],
                              a_.scf[:, tsl], start=True, stop=True)
                sm.tT.inc(mm)
                if e == 0:
                    ring_x6.wait_upto(t, tg, per=16)
                for ht in range(16):
                    i = (tg * 8 + e) * 16 + ht
                    ring_w1.wait_upto(t, i, per=1)
                    if ht == 2:
                        t.wait_ge(sm.tV.h, v2p(tg, "esb%d" % e))
                    if ht >= 4:
                        t.wait_ge(sm.tV.h, v2p(tg, "eha%d_%d" % (e, ht - 4)))
                    elif e > 0:
                        t.wait_ge(sm.tV.h, v2p(tg, "eha%d_%d" % (e - 1, 12 + ht)))
                    elif tg > 0:
                        t.wait_ge(sm.tV.h, v2p(tg - 1, "eha7_%d" % (12 + ht)))
                    n3 = 0
                    for kt in range(8):
                        for (wi, xi) in [(0, 0), (0, 1), (1, 0)]:
                            n3 += 1
                            mm = t.matmul(ps[ht % 4][:, :], a_.w1b[:, i % 2, wi, kt, :],
                                          a_.xp[:, xi, kt, :], start=(n3 == 1),
                                          stop=(n3 == 24))
                    sm.tT.inc(mm)
                for ot in range(8):
                    gi = (tg * 8 + e) * 8 + ot
                    if ot >= 4:
                        t.wait_ge(sm.tV.h, v2p(tg, "ea%d_%d" % (e, ot - 4)))
                    elif e > 0:
                        t.wait_ge(sm.tV.h, v2p(tg, "ea%d_%d" % (e - 1, ot + 4)))
                    else:
                        t.wait_ge(sm.tV.h, v2p(tg, "eb%d" % (ot + 4)))
                    n3 = 0
                    for hkt in range(16):
                        h2 = hkt // 8
                        if hkt % 8 == 0:
                            ring_w2.wait_upto(t, 2 * gi + h2, per=1)
                        if ot == 0:
                            t.wait_ge(sm.tV.h, v2p(tg, "ehb%d_%d" % (e, hkt)))
                        for (wi, xi) in [(0, 0), (0, 1), (1, 0)]:
                            n3 += 1
                            hsx = a_.hsh if xi == 0 else a_.hsl
                            mm = t.matmul(ps[4 + ot % 4][:, :],
                                          a_.w2b[:, (2 * gi + h2) % 4, wi, hkt % 8, :],
                                          hsx[:, hkt, :], start=(n3 == 1),
                                          stop=(n3 == 48))
                        if hkt == 7:
                            sm.tT.inc(mm)
                    sm.tT.inc(mm)

    @block.vector
    def _(v):
        for tg in range(4):
            for ot in range(8):
                v.wait_ge(sm.tT.h, t2p(tg, "b2_%d" % ot))
                v.tensor_copy(a_.acc[:, ot, :], ps[4 + ot % 4][:, :])
                sm.tV.inc(v.drain())
            for e in range(8):
                v.wait_ge(sm.tT.h, t2p(tg, "sb%d" % e))
                v.tensor_copy(a_.scob[:, :], ps[2][:, :])
                sm.tV.inc(v.drain())
                if e > 0:
                    v.wait_ge(sm.tT.h, t2p(tg, "w2_%d_7" % (e - 1)))
                elif tg > 0:
                    v.wait_ge(sm.tT.h, t2p(tg - 1, "w2_7_7"))
                for ht in range(16):
                    if ht >= 1:
                        v.wait_ge(sm.tS.h, s2p(tg, "rl%d_%d" % (e, ht - 1)))
                        v.tensor_copy(a_.hsh[:, ht - 1, :],
                                      a_.hstage[:, (ht - 1) % 2, :])
                    v.wait_ge(sm.tT.h, t2p(tg, "w1_%d_%d" % (e, ht)))
                    if ht >= 2:
                        v.wait_ge(sm.tS.h, s2p(tg, "rl%d_%d" % (e, ht - 2)))
                    elif e > 0:
                        v.wait_ge(sm.tS.h, s2p(tg, "rl%d_%d" % (e - 1, 14 + ht)))
                    elif tg > 0:
                        v.wait_ge(sm.tS.h, s2p(tg - 1, "rl7_%d" % (14 + ht)))
                    v.scalar_tensor_tensor(out=a_.hstage[:, ht % 2, :],
                                           in0=ps[ht % 4][:, :],
                                           scalar=a_.b1b[:, e, ht:ht+1],
                                           in1=a_.scob[:, :],
                                           op0=ALU.add, op1=ALU.mult)
                    sm.tV.inc(v.drain())
                    if ht >= 1:
                        v.tensor_tensor(out=a_.hsl[:, ht - 1, :],
                                        in0=a_.hstage[:, (ht - 1) % 2, :],
                                        in1=a_.hsh[:, ht - 1, :], op=ALU.subtract)
                        sm.tV.inc(v.drain())
                v.wait_ge(sm.tS.h, s2p(tg, "rl%d_15" % e))
                v.tensor_copy(a_.hsh[:, 15, :], a_.hstage[:, 15 % 2, :])
                v.drain()
                v.tensor_tensor(out=a_.hsl[:, 15, :], in0=a_.hstage[:, 15 % 2, :],
                                in1=a_.hsh[:, 15, :], op=ALU.subtract)
                sm.tV.inc(v.drain())
                for ot in range(8):
                    v.wait_ge(sm.tT.h, t2p(tg, "w2_%d_%d" % (e, ot)))
                    v.tensor_tensor(out=a_.acc[:, ot, :], in0=ps[4 + ot % 4][:, :],
                                    in1=a_.acc[:, ot, :], op=ALU.add)
                    sm.tV.inc(v.drain())
            for pt in range(8):
                i2 = tg * 8 + pt
                ring_m.wait_upto(v, i2, per=5)
                v.wait_ge(sm.tT.h, t2p(tg, "w2_7_%d" % pt))
                v.tensor_tensor(out=a_.xc[:, i2 % 2, :], in0=a_.xc[:, i2 % 2, :],
                                in1=a_.ac[:, i2 % 2, :], op=ALU.add)
                v.drain()
                v.tensor_tensor(out=a_.xc[:, i2 % 2, :], in0=a_.xc[:, i2 % 2, :],
                                in1=a_.acc[:, pt, :], op=ALU.add)
                sm.tV.inc(v.drain())

    @block.scalar
    def _(sc):
        for tg in range(4):
            for e in range(8):
                for ht in range(16):
                    sc.wait_ge(sm.tV.h, v2p(tg, "eha%d_%d" % (e, ht)))
                    sm.tS.inc(sc.activation(out=a_.hstage[:, ht % 2, :],
                                            in_=a_.hstage[:, ht % 2, :],
                                            func=AF.Relu))

    tap(ct, "p6_scf", lambda: a_.scf[:, :])
    ar.pop()




def kernel(**inputs):
    from concourse.bass_utils import run_bass_kernel_spmd
    nc, ct = build(taps=(), ncores=NCORES)
    shared = stage_shared(inputs)
    in_maps = []
    for c in range(NCORES):
        m = dict(shared)
        m.update(stage_core(inputs, c))
        in_maps.append(m)
    res = run_bass_kernel_spmd(nc, in_maps, core_ids=list(range(NCORES)))
    return unstage_out(res.results)
